# version 09
# baseline (speedup 1.0000x reference)
import math
import numpy as np

import concourse.bass as bass
import concourse.mybir as mybir
from concourse.bass_utils import run_bass_kernel_spmd

F32 = mybir.dt.float32
BF16 = mybir.dt.bfloat16
AF = mybir.ActivationFunctionType
ALU = mybir.AluOpType
AXX = mybir.AxisListType.X

D = 1024
H = 16
HD = 64
NE = 32
HF = 256
NCORES = 8
LN2C = float(math.log(2.0))

ENGINES = ("tensor", "scalar", "vector", "gpsimd", "sync")
SEMS = ["mm", "sc", "ve", "dg", "dy", "dc", "dw", "st", "wa", "wb", "wc", "wd2"]
QUEUE_OF = {"dg": "gpsimd", "dy": "sync", "dc": "scalar", "dw": "scalar", "st": "sync",
            "wa": "sync", "wb": "sync", "wc": "gpsimd", "wd2": "gpsimd"}
Z = ("ve", 0)


def host_constants(T):
    c = {}
    c["cIdent"] = np.eye(128, dtype=np.float32)
    P = np.zeros((128, 128), dtype=np.float32)
    for k in range(64):
        P[2 * k, 2 * k + 1] = -1.0
        P[2 * k + 1, 2 * k] = 1.0
    c["cPerm"] = np.ascontiguousarray(P.T)
    t64 = np.arange(T, dtype=np.float64)
    j64 = np.arange(0, HD, 2, dtype=np.float64)
    inv_freq = 1.0 / (10000.0 ** (j64 / HD))
    ang = t64[:, None] * inv_freq[None, :]
    cos64 = np.cos(np.concatenate([ang, ang], axis=1))
    sin64 = np.sin(np.concatenate([ang, ang], axis=1))
    c["cCos"] = np.tile(cos64.T, (2, 1)).astype(np.float32)
    c["cSin"] = np.tile(sin64.T, (2, 1)).astype(np.float32)
    c["cJ"] = np.broadcast_to(np.arange(1024, dtype=np.float32),
                              (128, 1024)).copy()
    p = np.arange(128, dtype=np.float32)
    c["cPCM"] = (128.0 * np.arange(32, dtype=np.float32)[None, :]
                 + p[:, None]).astype(np.float32)
    c["cOnes"] = np.ones((1, 128), dtype=np.float32)
    c["cSc"] = np.broadcast_to(np.array([-5.0 * LN2C, 1e-5],
                                        dtype=np.float32), (128, 2)).copy()
    c["cInd0"] = (p < 64).astype(np.float32)[:, None]
    c["cInd1"] = (p >= 64).astype(np.float32)[:, None]
    c["cIotaE"] = np.arange(NE, dtype=np.float32)[:, None]
    return c


CONST_SPECS = {
    "cIdent": (lambda T: [128, 128], BF16),
    "cPerm": (lambda T: [128, 128], BF16),
    "cCos": (lambda T: [128, T], F32),
    "cSin": (lambda T: [128, T], F32),
    "cJ": (lambda T: [128, 1024], F32),
    "cPCM": (lambda T: [128, 32], F32),
    "cSc": (lambda T: [128, 2], F32),
    "cOnes": (lambda T: [1, 128], BF16),
    "cInd0": (lambda T: [128, 1], F32),
    "cInd1": (lambda T: [128, 1], F32),
    "cIotaE": (lambda T: [32, 1], BF16),
}
STREAMED_CONSTS = ("cCos", "cSin")


class Prog:

    def __init__(self, cnt=None, group_end=None):
        self.cnt = dict(cnt) if cnt else {s: 0 for s in SEMS}
        self.group_end = dict(group_end) if group_end else {}
        self.ops = {e: [] for e in ENGINES}

    def op(self, eng, fn, waits=(), inc=None, incby=1):
        w = [(s, v) for (s, v) in waits if v > 0]
        self.ops[eng].append((fn, w, inc, incby))
        if inc is not None:
            self.cnt[inc] += incby
        return self.tok(inc) if inc else None

    def dma(self, q, out, in_, transpose=False, waits=(), slow=False):
        w = list(waits) + [(q, self.group_end.get(q, 0))]
        if transpose:
            fn = (lambda e, out=out, in_=in_:
                  e.dma_start(out=out, in_=in_, transpose=True))
        else:
            fn = (lambda e, out=out, in_=in_, slow=slow: e.dma_start(
                out=out, in_=in_, allow_slow_non_contiguous=slow))
        return self.op(QUEUE_OF[q], fn, waits=w, inc=q, incby=16)

    def mark(self, q):
        self.group_end[q] = self.cnt[q]
        return (q, self.cnt[q])

    def tok(self, sem):
        return (sem, self.cnt[sem])

    def emit(self, nc, sems, name):
        with nc.Block(name) as blk:
            for eng_name in ENGINES:
                ops = self.ops[eng_name]
                fin = [(s, self.cnt[s]) for s, q in QUEUE_OF.items()
                       if q == eng_name and self.cnt[s] > 0]
                if not ops and not fin:
                    continue

                def body(e, ops=ops, fin=fin):
                    last_wait = {}
                    for (fn, waits, inc, incby) in ops:
                        for (s, v) in waits:
                            if last_wait.get(s, -1) >= v:
                                continue
                            e.wait_ge(sems[s], v)
                            last_wait[s] = v
                        ins = fn(e)
                        if inc is not None:
                            ins.then_inc(sems[inc], incby)
                    for (s, v) in fin:
                        if last_wait.get(s, -1) < v:
                            e.wait_ge(sems[s], v)
                getattr(blk, eng_name)(body)
        return dict(self.cnt)


class Carver:
    def __init__(self, tensor, nelem):
        self.t = tensor
        self.n = nelem
        self.off = 0

    def take(self, shape, dt):
        n = int(np.prod(shape[1:]))
        if dt == F32:
            if self.off % 2:
                self.off += 1
            n *= 2
        v = self.t[:shape[0], self.off:self.off + n]
        self.off += n
        assert self.off <= self.n, f"carve overflow {self.off} > {self.n}"
        if dt == F32:
            v = v.bitcast(F32)
        if len(shape) == 3:
            v = v.rearrange("p (a b) -> p a b", a=shape[1], b=shape[2])
        elif len(shape) == 4:
            v = v.rearrange("p (a b c) -> p a b c", a=shape[1], b=shape[2],
                            c=shape[3])
        return v


def build_program(T=4096, detect_races=True):
    nc = bass.Bass("TRN2", target_bir_lowering=False,
                   detect_race_conditions=detect_races)
    NTB = T // 512
    NQ4 = T // 1024
    NTG = T // 256
    NKT = T // 128

    ap = {}
    ap["x"] = nc.dram_tensor("x", [T, D], F32, kind="ExternalInput").ap()
    for w in ["wq", "wk", "wv", "wg", "wo"]:
        ap[w] = nc.dram_tensor(w, [D, D], F32, kind="ExternalInput").ap()
    for b in ["bq", "bk", "bv", "bg", "bo", "gamma", "beta"]:
        ap[b] = nc.dram_tensor(b, [D], F32, kind="ExternalInput").ap()
    ap["wd"] = nc.dram_tensor("wd", [H, D], F32, kind="ExternalInput").ap()
    ap["bd"] = nc.dram_tensor("bd", [H], F32, kind="ExternalInput").ap()
    ap["rw"] = nc.dram_tensor("rw", [NE, D], F32, kind="ExternalInput").ap()
    ap["rb"] = nc.dram_tensor("rb", [NE], F32, kind="ExternalInput").ap()
    ap["w1"] = nc.dram_tensor("w1", [NE, D, HF], F32, kind="ExternalInput").ap()
    ap["b1"] = nc.dram_tensor("b1", [NE, HF], F32, kind="ExternalInput").ap()
    ap["w2"] = nc.dram_tensor("w2", [NE, HF, D], F32, kind="ExternalInput").ap()
    ap["b2"] = nc.dram_tensor("b2", [NE, D], F32, kind="ExternalInput").ap()
    for name, (shfn, dt) in CONST_SPECS.items():
        ap[name] = nc.dram_tensor(name, shfn(T), dt, kind="ExternalInput").ap()
    out = nc.dram_tensor("out", [T, D], F32, kind="ExternalOutput").ap()

    sc = {}
    for w in ["wq", "wk", "wv", "wg", "wo"]:
        sc[w] = nc.dram_tensor(w + "_bf", [D, D], BF16, kind="Internal").ap()
    sc["rw"] = nc.dram_tensor("rw_bf", [NE, D], BF16, kind="Internal").ap()
    sc["wd"] = nc.dram_tensor("wd_bf", [H, D], BF16, kind="Internal").ap()
    sc["w1"] = nc.dram_tensor("w1_bf", [NE, D, HF], BF16, kind="Internal").ap()
    sc["w2"] = nc.dram_tensor("w2_bf", [NE, HF, D], BF16, kind="Internal").ap()
    sc["qt"] = nc.dram_tensor("qt_sc", [D, T], BF16, kind="Internal").ap()
    sc["gt"] = nc.dram_tensor("gt_sc", [D, T], BF16, kind="Internal").ap()
    sc["ot"] = nc.dram_tensor("ot_sc", [D, T], BF16, kind="Internal").ap()
    sc["dd"] = nc.dram_tensor("d_sc", [H * NQ4, 1024], BF16,
                              kind="Internal").ap()
    sc["rr"] = nc.dram_tensor("r_sc", [H * NQ4, 1024], BF16,
                              kind="Internal").ap()
    for w in ["wq", "wk", "wv", "wg", "wo"]:
        sc[w + "T"] = nc.dram_tensor(w + "T_bf", [D, D], BF16,
                                     kind="Internal").ap()

    from contextlib import ExitStack
    es = ExitStack()

    def sb(name, shape, dt):
        return es.enter_context(nc.sbuf_tensor(name, shape, dt))

    sems = {s: es.enter_context(nc.semaphore("sem_" + s)) for s in SEMS}

    big = sb("big", [128, 64 * 512], BF16)
    ktb = sb("ktb", [128, 8, max(T, 4096)], BF16)
    vp = sb("vp", [128, max(T, 4096) // 128, H, 65], BF16)
    cb = {}
    for name, (shfn, dt) in CONST_SPECS.items():
        if name in STREAMED_CONSTS:
            continue
        cb[name] = sb(name + "_s", shfn(T), dt)
    pp = {}
    for b in ["bq", "bk", "bg", "gamma", "beta"]:
        pp[b] = sb(b + "_pp", [128, 8], F32)
    bo_row = sb("bo_row", [1, D], BF16)
    rb_row = sb("rb_row", [1, NE], BF16)
    bd_pp = sb("bd_pp", [16, 1], F32)
    b1_pp = sb("b1_pp", [128, NE, 2], F32)
    ldB = sb("ldB", [128, 16], F32)
    rall = sb("rall", [H * (T // 1024), 1024], BF16)
    gnsc = sb("gnsc", [128, 8], F32)
    gnbi = sb("gnbi", [128, 8], F32)
    xm = sb("xm", [128, 8], F32)
    stat = sb("stat", [128, 16], F32)
    smal = sb("smal", [128, 64], F32)
    smbf = sb("smbf", [128, 64], BF16)
    srow = sb("srow", [1, 64], F32)
    srbf = sb("srbf", [1, 64], BF16)

    x2v = big[:, :].rearrange("p (tc d) -> p tc d", d=D)
    ktflat = ktb[:, :, :].rearrange("p a b -> p (a b)")
    nc.const_aps.register(F32, -5.0 * LN2C, cb["cSc"][:, 0:1])
    nc.const_aps.register(F32, 1e-5, cb["cSc"][:, 1:2])

    P = Prog()
    for w in ["wq", "wk", "wv", "wg", "wo"]:
        for hh in range(2):
            P.dma("dg", sc[w][512 * hh:512 * (hh + 1), :],
                  ap[w][512 * hh:512 * (hh + 1), :])
    P.dma("dg", sc["rw"][:, :], ap["rw"][:, :])
    P.dma("dg", sc["wd"][:, :], ap["wd"][:, :])
    P.dma("dg", bo_row[0:1, :], ap["bo"].rearrange("(o d) -> o d", o=1))
    P.dma("dg", rb_row[0:1, :], ap["rb"].rearrange("(o d) -> o d", o=1))
    tok_S_dg = P.mark("dg")
    tok_wstage = tok_S_dg
    for name in cb:
        P.dma("dy", cb[name][...], ap[name][...])
    for b in ["bq", "bk", "bg", "gamma", "beta"]:
        P.dma("dy", pp[b][:, :], ap[b].rearrange("(c p) -> p c", p=128),
              slow=True)
    P.dma("dy", bd_pp[:, :], ap["bd"].rearrange("(p o) -> p o", o=1),
          slow=True)
    P.dma("dy", b1_pp[:, :, :],
          ap["b1"].rearrange("e (h p) -> p e h", p=128), slow=True)
    tok_consts = P.mark("dy")
    cvS = Carver(big, 64 * 512)
    tbuf = cvS.take([128, 2, 1024], BF16)
    tb_free = [Z, Z]
    ti = 0
    for w in ["wq", "wk", "wv", "wg", "wo"]:
        for dc in range(8):
            par = ti % 2
            P.dma("dc", tbuf[:, par, :],
                  sc[w][:, 128 * dc:128 * (dc + 1)], transpose=True,
                  waits=[tok_wstage, tb_free[par]])
            t_l = P.mark("dc")
            P.dma("dy", sc[w + "T"][128 * dc:128 * (dc + 1), :],
                  tbuf[:, par, :], waits=[t_l])
            tb_free[par] = P.mark("dy")
            ti += 1
    tok_wT = P.tok("dy")
    cnt = P.emit(nc, sems, "blkS")
    ge = dict(P.group_end)

    P = Prog(cnt, ge)
    esA = ExitStack()
    cv = Carver(big, 64 * 512)
    xrow = cv.take([128, 4, D], BF16)
    xt = cv.take([128, 8, 512], BF16)
    wts = cv.take([128, 2, 8, D], BF16)
    qts = cv.take([128, 2, 512], BF16)
    qtr = cv.take([128, 2, 512], BF16)
    rt1 = cv.take([128, 512], BF16)
    rt2 = cv.take([128, 512], BF16)
    cosl = cv.take([128, 2, 512], F32)
    sinl = cv.take([128, 2, 512], F32)
    bv_row = cv.take([1, D], BF16)
    wdt = esA.enter_context(nc.sbuf_tensor("wdt", [128, 8, 16], BF16))

    ptx = [esA.enter_context(nc.psum_tensor(f"ptx{i}", [128, 128], BF16))
           for i in range(2)]
    pq = [esA.enter_context(nc.psum_tensor(f"pq{i}", [128, 512], F32))
          for i in range(2)]
    pr = esA.enter_context(nc.psum_tensor("pr", [128, 512], F32))
    pv = esA.enter_context(nc.psum_tensor("pv", [128, D], F32))

    P.op("vector", lambda e: e.memset(vp[:, :, :, :], 1.0), inc="ve")
    P.op("vector", lambda e: e.memset(xm[:, :], 0.0), inc="ve")
    P.op("vector", lambda e: e.memset(stat[:, :], 0.0), inc="ve")
    P.op("vector", lambda e: e.memset(smbf[:, :], 0.0), inc="ve")
    P.op("vector",
         lambda e: e.tensor_scalar(smal[:, 52:60], pp["bq"][:, :], 0.125,
                                   None, ALU.mult),
         waits=[tok_consts], inc="ve")
    t_init = P.tok("ve")
    P.dma("dg", bv_row[0:1, :], ap["bv"].rearrange("(o d) -> o d", o=1),
          waits=[tok_S_dg])
    t_bv = P.mark("dg")
    for dc in range(8):
        P.dma("dw", wdt[:, dc, :], sc["wd"][:, 128 * dc:128 * (dc + 1)],
              transpose=True, waits=[tok_wstage])
    t_wdt = P.mark("dw")

    WL = ["q", "k", "g", "v"]
    chain_done = {}
    wts_tok = {}
    for ci in range(2):
        gate = [tok_wT]
        for dc in range(8):
            P.dma("dc", wts[:, ci % 2, dc, :],
                  sc["w" + WL[ci] + "T"][128 * dc:128 * (dc + 1), :],
                  waits=gate)
        wts_tok[ci] = P.mark("dc")

    evac_tok = {}
    xtr_done = {}
    pq_free = [Z, Z]
    qts_free = [[], []]
    qtr_free = [Z, Z]
    rt_last = Z
    pr_free = Z
    pv_free = Z
    cos_used = {}

    for tb in range(NTB):
        cpar = tb % 2
        xw = [t_init] + ([xtr_done[tb - 1]] if tb >= 1 else [])
        for ts_ in range(4):
            P.dma("dg", xrow[:, ts_, :],
                  ap["x"][512 * tb + 128 * ts_:512 * tb + 128 * (ts_ + 1), :],
                  waits=xw)
        t_x = P.mark("dg")
        cw = [cos_used[tb - 2]] if tb >= 2 else [tok_consts]
        P.dma("dy", cosl[:, cpar, :],
              ap["cCos"][:, 512 * tb:512 * (tb + 1)], waits=cw)
        P.dma("dy", sinl[:, cpar, :],
              ap["cSin"][:, 512 * tb:512 * (tb + 1)])
        t_cs = P.mark("dy")
        for i in range(32):
            dc, ts_ = i % 8, i // 8
            tw = [t_x, tok_consts]
            if (tb, i - 2) in evac_tok:
                tw.append(evac_tok[(tb, i - 2)])
            elif tb > 0:
                tw.append(evac_tok[(tb - 1, 30 + i)])
            tpi = i % 2
            t_mm = P.op("tensor",
                        lambda e, tpi=tpi, ts_=ts_, dc=dc: e.transpose(
                            ptx[tpi][:, :],
                            xrow[:, ts_, 128 * dc:128 * (dc + 1)],
                            cb["cIdent"][:, :]),
                        waits=tw, inc="mm")
            ew = [t_mm]
            if tb > 0:
                ew.append(chain_done[4 * tb - 1])
            evac_tok[(tb, i)] = P.op(
                "vector",
                lambda e, tpi=tpi, dc=dc, ts_=ts_: e.tensor_copy(
                    xt[:, dc, 128 * ts_:128 * (ts_ + 1)], ptx[tpi][:, :]),
                waits=ew, inc="ve")
        xtr_done[tb] = P.tok("mm")
        t_xt = P.tok("ve")
        for dc in range(8):
            P.op("vector",
                 lambda e, dc=dc: e.reduce_sum(smal[:, dc:dc + 1],
                                               xt[:, dc, :], axis=AXX),
                 waits=[t_xt], inc="ve")
        t_red = P.tok("ve")
        P.op("vector",
             lambda e: e.tensor_tensor(xm[:, :], xm[:, :], smal[:, 0:8],
                                       ALU.add),
             waits=[t_red], inc="ve")
        for iw, w in enumerate(WL):
            ci = 4 * tb + iw
            par = ci % 2
            if w in ("q", "k", "g"):
                rot_pend = []

                def do_rot(ec, qpar, t_ev, w=w, tb=tb, cpar=cpar):
                    global_ns = None
                    t_rot = P.op(
                        "tensor",
                        lambda e, qpar=qpar: e.matmul(
                            pr[:, :], cb["cPerm"][:, :], qts[:, qpar, :],
                            start=True, stop=True),
                        waits=[t_ev, rot_state["pr_free"]], inc="mm")
                    t_r1 = P.op(
                        "vector",
                        lambda e, cpar=cpar: e.tensor_tensor(
                            rt1[:, :], pr[:, :],
                            sinl[:, cpar, :], ALU.mult),
                        waits=[t_rot, t_cs, rot_state["rt_last"]], inc="ve")
                    rot_state["pr_free"] = t_r1
                    t_r2 = P.op(
                        "vector",
                        lambda e, qpar=qpar, cpar=cpar:
                        e.tensor_tensor(rt2[:, :], qts[:, qpar, :],
                                        cosl[:, cpar, :], ALU.mult),
                        waits=[t_ev, rot_state["rt_last"]], inc="ve")
                    qts_free[qpar] = [t_r2]
                    if w == "k":
                        rt_l = P.op(
                            "vector",
                            lambda e, ec=ec, tb=tb: e.tensor_tensor(
                                ktb[:, ec, 512 * tb:512 * (tb + 1)],
                                rt1[:, :], rt2[:, :], ALU.add),
                            waits=[t_r1, t_r2], inc="ve")
                    else:
                        dpar = ec % 2
                        rt_l = P.op(
                            "vector",
                            lambda e, dpar=dpar: e.tensor_tensor(
                                qtr[:, dpar, :], rt1[:, :], rt2[:, :],
                                ALU.add),
                            waits=[t_r1, t_r2, qtr_free[dpar]], inc="ve")
                        P.dma(
                            "dy",
                            sc["qt"][128 * ec:128 * (ec + 1),
                                     512 * tb:512 * (tb + 1)],
                            qtr[:, dpar, :], waits=[rt_l])
                        qtr_free[dpar] = P.mark("dy")
                    rot_state["rt_last"] = rt_l

                rot_state = {"pr_free": pr_free, "rt_last": rt_last}
                for ec in range(8):
                    ppar = ec % 2

                    def mmfn(e, par=par, ppar=ppar, ec=ec):
                        last = None
                        for dc in range(8):
                            last = e.matmul(
                                pq[ppar][:, :],
                                wts[:, par, dc, 128 * ec:128 * (ec + 1)],
                                xt[:, dc, :],
                                start=(dc == 0), stop=(dc == 7))
                        return last
                    t_mm = P.op("tensor", mmfn,
                                waits=[wts_tok[ci], t_xt, pq_free[ppar]],
                                inc="mm")
                    func = AF.Silu if w == "g" else AF.Identity
                    if w == "q":
                        bias = smal[:, 52 + ec:53 + ec]
                        scl = 0.125
                    else:
                        bias = pp["b" + w][:, ec:ec + 1]
                        scl = 1.0
                    qpar = ec % 2
                    t_ev = P.op("scalar",
                                lambda e, qpar=qpar, ppar=ppar, func=func,
                                bias=bias, scl=scl: e.activation(
                                    out=qts[:, qpar, :], in_=pq[ppar][:, :],
                                    func=func, bias=bias, scale=scl),
                                waits=[t_mm, tok_consts, t_init]
                                + qts_free[qpar],
                                inc="sc")
                    pq_free[ppar] = t_ev
                    if w == "g":
                        P.dma("dy",
                              sc["gt"][128 * ec:128 * (ec + 1),
                                       512 * tb:512 * (tb + 1)],
                              qts[:, qpar, :], waits=[t_ev])
                        qts_free[qpar] = [P.mark("dy")]
                    else:
                        rot_pend.append((ec, qpar, t_ev))
                        if len(rot_pend) > 1:
                            do_rot(*rot_pend.pop(0))
                while rot_pend:
                    do_rot(*rot_pend.pop(0))
                pr_free = rot_state["pr_free"]
                rt_last = rot_state["rt_last"]
                cos_used[tb] = P.tok("ve")
            else:
                for ts_ in range(4):
                    def vmm(e, par=par, ts_=ts_):
                        for dc in range(8):
                            for nh in range(2):
                                e.matmul(pv[:, 512 * nh:512 * (nh + 1)],
                                         xt[:, dc, 128 * ts_:128 * (ts_ + 1)],
                                         wts[:, par, dc,
                                             512 * nh:512 * (nh + 1)],
                                         start=(dc == 0), stop=False)
                        last = None
                        for nh in range(2):
                            last = e.matmul(pv[:, 512 * nh:512 * (nh + 1)],
                                            cb["cOnes"][0:1, :],
                                            bv_row[0:1,
                                                   512 * nh:512 * (nh + 1)],
                                            start=False, stop=True)
                        return last
                    t_mm = P.op("tensor", vmm,
                                waits=[wts_tok[ci], t_xt, t_bv, pv_free],
                                inc="mm")
                    kt = 4 * tb + ts_
                    pv_free = P.op(
                        "vector",
                        lambda e, kt=kt: e.tensor_copy(
                            vp[:, kt, :, 0:64],
                            pv[:, :].rearrange("p (h d) -> p h d", h=H)),
                        waits=[t_mm, t_init], inc="ve")
            chain_done[ci] = P.tok("mm")
            if ci + 2 < 4 * NTB:
                nci = ci + 2
                nw = WL[nci % 4]
                for dc in range(8):
                    P.dma("dc", wts[:, nci % 2, dc, :],
                          sc["w" + nw + "T"][128 * dc:128 * (dc + 1), :],
                          waits=[chain_done[ci], tok_wT])
                wts_tok[nci] = P.mark("dc")

    t_a = P.op("vector",
               lambda e: e.tensor_scalar(smbf[:, 0:8], xm[:, :], 1.0 / T,
                                         None, ALU.mult),
               waits=[P.tok("ve")], inc="ve")

    def pdmm(e):
        last = None
        for dc in range(8):
            last = e.matmul(pq[0][0:16, 0:1], wdt[:, dc, :],
                            smbf[:, dc:dc + 1],
                            start=(dc == 0), stop=(dc == 7))
        return last
    t_pd = P.op("tensor", pdmm, waits=[t_a, t_wdt, pq_free[0]], inc="mm")
    t_b = P.op("vector",
               lambda e: e.tensor_tensor(smal[0:16, 10:11], pq[0][0:16, 0:1],
                                         bd_pp[:, :], ALU.add),
               waits=[t_pd, tok_consts], inc="ve")
    t_c = P.op("scalar",
               lambda e: e.activation(out=smal[0:16, 9:10],
                                      in_=smal[0:16, 10:11],
                                      func=AF.Sigmoid),
               waits=[t_b], inc="sc")
    t_d = P.op("scalar",
               lambda e: e.activation(out=smal[0:16, 11:12],
                                      in_=smal[0:16, 9:10], func=AF.Exp,
                                      bias=-5.0 * LN2C, scale=-8.0 * LN2C),
               waits=[t_c], inc="sc")
    t_e = P.op("vector",
               lambda e: e.tensor_scalar(smal[0:16, 13:14],
                                         smal[0:16, 11:12], -1.0, 1.0,
                                         ALU.mult, ALU.add),
               waits=[t_d], inc="ve")
    t_f = P.op("scalar",
               lambda e: e.activation(out=smal[0:16, 12:13],
                                      in_=smal[0:16, 13:14], func=AF.Ln),
               waits=[t_e], inc="sc")
    t_g = P.op("vector",
               lambda e: e.tensor_copy(smbf[0:16, 8:9], smal[0:16, 12:13]),
               waits=[t_f], inc="ve")
    t_h = P.op("tensor",
               lambda e: e.transpose(ptx[0][0:1, 0:128], smbf[:, 8:9],
                                     cb["cIdent"][:, :]),
               waits=[t_g, evac_tok[(NTB - 1, 31)]], inc="mm")
    t_i = P.op("vector",
               lambda e: e.tensor_copy(srbf[0:1, 0:16], ptx[0][0:1, 0:16]),
               waits=[t_h], inc="ve")
    t_j = P.op("tensor",
               lambda e: e.matmul(pq[1][:, 0:16], cb["cOnes"][0:1, :],
                                  srbf[0:1, 0:16], start=True, stop=True),
               waits=[t_i, pq_free[1]], inc="mm")
    P.op("vector", lambda e: e.tensor_copy(ldB[:, :], pq[1][:, 0:16]),
         waits=[t_j], inc="ve")
    cnt = P.emit(nc, sems, "blkA")
    ge = dict(P.group_end)
    esA.close()

    P = Prog(cnt, ge)
    esB = ExitStack()
    cv = Carver(big, 64 * 512)
    qt_sb = cv.take([128, 2, 1024], BF16)
    ptraw = cv.take([128, 2, 1024], BF16)
    ptfin = cv.take([128, 4, 1024], BF16)
    hBt = cv.take([128, 1024], BF16)
    otm = cv.take([128, 2, 1024], BF16)
    ut = cv.take([128, 2, 1024], F32)
    lJ = cv.take([128, 1024], F32)
    tm = cv.take([128, 1024], F32)
    cJl = cv.take([128, 1024], F32)
    bvD = cv.take([128, 32], F32)
    dvq = cv.take([128, 2, 32], F32)
    dall = cv.take([H * NQ4, 1024], BF16)
    lnd = cv.take([H * NQ4, 1024], F32)

    Sp = [esB.enter_context(nc.psum_tensor(f"S{i}", [128, 1024], F32))
          for i in range(2)]
    pOT = [esB.enter_context(nc.psum_tensor(f"pOT{i}", [65, 1024], F32))
           for i in range(2)]

    P.dma("dy", cJl[:, :], ap["cJ"][:, :], waits=[tok_consts])
    tok_cj = P.mark("dy")
    for e_ in range(NE):
        for cc in range(8):
            P.dma("dg",
                  sc["w1"][e_, :, :].rearrange("(p c) f -> c p f", c=8)[cc],
                  ap["w1"][e_, 128 * cc:128 * (cc + 1), :], slow=True)
        for cc in range(2):
            P.dma("dg",
                  sc["w2"][e_, :, :].rearrange("(p c) f -> c p f", c=2)[cc],
                  ap["w2"][e_, 128 * cc:128 * (cc + 1), :], slow=True)
    tok_ffn = P.mark("dg")

    t_ldB = P.tok("ve")
    qt_tok = {}
    P.dma("dy", qt_sb[:, 0, :], sc["qt"][0:128, 0:1024])
    qt_tok[0] = P.mark("dy")

    S_free = [Z, Z]
    pt_free = [Z, Z, Z, Z]
    praw_free = [Z, Z]
    ut_free = [Z, Z]
    otm_free = [Z, Z]
    pOT_free = [Z, Z]
    tm_free = Z
    dvq_free = [[], []]
    prep_free = {"bvD": Z, "lJ": [], "hB": Z}
    scores_done = {0: Z, 1: Z}
    h_prep = {}

    NG = 8 * NQ4
    for g in range(NG):
        pr_, qb4 = divmod(g, NQ4)
        gpar = g % 2
        if g + 1 < NG:
            npr, nqb = divmod(g + 1, NQ4)
            P.dma("dy", qt_sb[:, (g + 1) % 2, :],
                  sc["qt"][128 * npr:128 * (npr + 1),
                           1024 * nqb:1024 * (nqb + 1)],
                  waits=[scores_done[(g + 1) % 2], qt_tok[g]])
            qt_tok[g + 1] = P.mark("dy")
        for sub in range(2):
            h = 2 * pr_ + sub
            if qb4 == 0:
                t_bvd = P.op("vector",
                             lambda e, h=h: e.tensor_scalar(
                                 bvD[:, :], cb["cPCM"][:, :],
                                 ldB[:, h:h + 1], None, ALU.mult),
                             waits=[prep_free["bvD"], tok_consts, t_ldB],
                             inc="ve")
                t_lj = P.op("vector",
                            lambda e, h=h: e.tensor_scalar(
                                lJ[:, :], cJl[:, :], ldB[:, h:h + 1], None,
                                ALU.mult),
                            waits=[tok_cj, t_ldB] + prep_free["lJ"],
                            inc="ve")
                t_hb = P.op("scalar",
                            lambda e: e.activation(out=hBt[:, :],
                                                   in_=lJ[:, :],
                                                   func=AF.Exp),
                            waits=[t_lj, prep_free["hB"]], inc="sc")
                h_prep[h] = (t_bvd, t_lj, t_hb)
            t_bvd, t_lj, t_hb = h_prep[h]
            dpar = (2 * g + sub) % 2
            t_q1 = P.op("vector",
                        lambda e, h=h, qb4=qb4: e.tensor_scalar(
                            smal[:, 20:21], ldB[:, h:h + 1],
                            -1024.0 * qb4, None, ALU.mult),
                        waits=[t_bvd], inc="ve")
            t_dvq = P.op("vector",
                         lambda e, dpar=dpar: e.tensor_scalar(
                             dvq[:, dpar, :], bvD[:, :], smal[:, 20:21],
                             None, ALU.add),
                         waits=[t_q1] + dvq_free[dpar], inc="ve")
            r0, r1 = (0, 64) if sub == 0 else (64, 128)
            tp = (0, 0) if sub == 0 else (64, 0)
            av_pend = []

            def emit_av(kt, par4, t_pt):
                def avfn(e, kt=kt, par4=par4, h=h):
                    last = None
                    for qh in range(2):
                        e.matmul(pOT[0][:, 512 * qh:512 * (qh + 1)],
                                 vp[0:64, kt, h, :],
                                 ptfin[0:64, par4, 512 * qh:512 * (qh + 1)],
                                 start=(kt == 0), stop=(kt == NKT - 1),
                                 tile_position=(0, 0))
                        last = e.matmul(
                            pOT[1][:, 512 * qh:512 * (qh + 1)],
                            vp[64:128, kt, h, :],
                            ptfin[64:128, par4, 512 * qh:512 * (qh + 1)],
                            start=(kt == 0), stop=(kt == NKT - 1),
                            tile_position=(64, 0))
                    return last
                avw = [t_pt]
                if kt == 0:
                    avw += pOT_free
                pt_free[par4] = P.op("tensor", avfn, waits=avw, inc="mm")

            for kt in range(NKT):
                par = kt % 2
                par4 = kt % 4
                ms = kt - 8 * qb4
                def scfn(e, pr_=pr_, kt=kt, par=par, tp=tp, r0=r0, r1=r1,
                         gpar=gpar):
                    last = None
                    for qh in range(2):
                        last = e.matmul(
                            Sp[par][:, 512 * qh:512 * (qh + 1)],
                            ktb[r0:r1, pr_, 128 * kt:128 * (kt + 1)],
                            qt_sb[r0:r1, gpar, 512 * qh:512 * (qh + 1)],
                            start=True, stop=True, tile_position=tp)
                    return last
                t_mm = P.op("tensor", scfn,
                            waits=[qt_tok[g], S_free[par]], inc="mm")
                if ms >= 8:
                    t_pt = P.op(
                        "scalar",
                        lambda e, par=par, par4=par4, dpar=dpar, kt=kt:
                        e.activation(
                            out=ptfin[:, par4, :], in_=Sp[par][:, :],
                            func=AF.Exp, bias=dvq[:, dpar, kt:kt + 1],
                            scale=1.0),
                        waits=[t_mm, t_dvq, pt_free[par4]], inc="sc")
                    S_free[par] = t_pt
                elif ms < 0:
                    t_ex = P.op(
                        "scalar",
                        lambda e, par=par: e.activation(
                            out=ptraw[:, par, :], in_=Sp[par][:, :],
                            func=AF.Exp),
                        waits=[t_mm, praw_free[par]], inc="sc")
                    S_free[par] = t_ex
                    t_pt = P.op(
                        "vector",
                        lambda e, par=par, par4=par4: e.tensor_tensor(
                            ptfin[:, par4, :], ptraw[:, par, :], hBt[:, :],
                            ALU.mult),
                        waits=[t_ex, t_hb, pt_free[par4]], inc="ve")
                    praw_free[par] = t_pt
                else:
                    t_ut = P.op(
                        "vector",
                        lambda e, par=par, dpar=dpar, kt=kt:
                        e.scalar_tensor_tensor(
                            ut[:, par, :], lJ[:, :], dvq[:, dpar, kt:kt + 1],
                            Sp[par][:, :], ALU.min, ALU.add),
                        waits=[t_mm, t_dvq, t_lj, ut_free[par]], inc="ve")
                    S_free[par] = t_ut
                    t_pt = P.op(
                        "scalar",
                        lambda e, par=par, par4=par4: e.activation(
                            out=ptfin[:, par4, :], in_=ut[:, par, :],
                            func=AF.Exp),
                        waits=[t_ut, pt_free[par4]], inc="sc")
                    ut_free[par] = t_pt
                av_pend.append((kt, par4, t_pt))
                if len(av_pend) > 2:
                    emit_av(*av_pend.pop(0))
            while av_pend:
                emit_av(*av_pend.pop(0))
            if sub == 1:
                scores_done[gpar] = P.tok("mm")
            t_av = P.tok("mm")
            t_tm = P.op("vector",
                        lambda e: e.tensor_copy(tm[0:65, :], pOT[0][:, :]),
                        waits=[t_av, tm_free], inc="ve")
            opar = (2 * g + sub) % 2
            t_mg = P.op("vector",
                        lambda e, opar=opar: e.scalar_tensor_tensor(
                            otm[0:65, opar, :], pOT[1][:, :], 1.0,
                            tm[0:65, :], ALU.bypass, ALU.add),
                        waits=[t_tm, otm_free[opar]], inc="ve")
            tm_free = t_mg
            pOT_free = [t_tm, t_mg]
            P.dma("dy", sc["ot"][64 * h:64 * (h + 1),
                                 1024 * qb4:1024 * (qb4 + 1)],
                  otm[0:64, opar, :], waits=[t_mg])
            P.dma("dy", sc["dd"][h * NQ4 + qb4:h * NQ4 + qb4 + 1, :],
                  otm[64:65, opar, :], waits=[t_mg])
            otm_free[opar] = P.mark("dy")
            dvq_free[dpar] = [P.tok("sc"), P.tok("ve")]
            if qb4 == NQ4 - 1:
                prep_free["bvD"] = t_dvq
                prep_free["lJ"] = [P.tok("ve"), t_hb]
                prep_free["hB"] = P.tok("ve")
    P.dma("dy", dall[:, :], sc["dd"][:, :], waits=[P.tok("dy")])
    t_da = P.mark("dy")
    t_l1 = P.op("scalar",
                lambda e: e.activation(out=lnd[:, :], in_=dall[:, :],
                                       func=AF.Ln),
                waits=[t_da], inc="sc")
    t_l2 = P.op("vector",
                lambda e: e.tensor_scalar(lnd[:, :], lnd[:, :], -1.0, None,
                                          ALU.mult),
                waits=[t_l1], inc="ve")
    t_l3 = P.op("scalar",
                lambda e: e.activation(out=rall[:, :], in_=lnd[:, :],
                                       func=AF.Exp),
                waits=[t_l2], inc="sc")
    P.dma("dy", sc["rr"][:, :], rall[:, :], waits=[t_l3])
    P.mark("dy")
    cnt = P.emit(nc, sems, "blkB")
    ge = dict(P.group_end)
    esB.close()

    P = Prog(cnt, ge)
    esB2 = ExitStack()
    cv = Carver(ktflat, 8 * max(T, 4096))
    otb = cv.take([128, 2, 1024], BF16)
    tst = cv.take([128, 1024], BF16)
    tsq = cv.take([128, 1024], F32)
    rbuf = cv.take([1, 2, 2, 1024], BF16)
    pRb = esB2.enter_context(nc.psum_tensor("pRb", [128, 1024], F32))
    pS1 = esB2.enter_context(nc.psum_tensor("pS1", [128, 16], F32))
    pS2 = esB2.enter_context(nc.psum_tensor("pS2", [1, 32], F32))

    rsc4 = sc["rr"].rearrange("(h q) d -> h q d", q=NQ4)

    def rb_build(Pp, pRbT, rbufs, waits):
        def rbfn(e, rbufs=rbufs):
            for nh in range(2):
                e.matmul(pRbT[0:64, 512 * nh:512 * (nh + 1)],
                         cb["cOnes"][0:1, 0:64],
                         rbufs[0:1, 0, 512 * nh:512 * (nh + 1)],
                         start=True, stop=True, tile_position=(0, 0))
            last = None
            for nh in range(2):
                last = e.matmul(pRbT[64:128, 512 * nh:512 * (nh + 1)],
                                cb["cOnes"][0:1, 0:64],
                                rbufs[0:1, 1, 512 * nh:512 * (nh + 1)],
                                start=True, stop=True,
                                tile_position=(0, 64))
            return last
        return Pp.op("tensor", rbfn, waits=waits, inc="mm")

    t_rall = P.tok("sc")
    rb_used = [Z, Z]
    otb_free = [Z, Z]
    ot_tok = {}
    P.dma("dy", otb[:, 0, :], sc["ot"][0:128, 0:1024], waits=[t_rall])
    ot_tok[0] = P.mark("dy")
    Rb_free = Z
    tst_free = Z
    tsq_free = Z
    t_acc = Z
    for g in range(8 * NQ4):
        c, qb4 = divmod(g, NQ4)
        par = g % 2
        if g + 1 < 8 * NQ4:
            nc_, nqb = divmod(g + 1, NQ4)
            P.dma("dy", otb[:, (g + 1) % 2, :],
                  sc["ot"][128 * nc_:128 * (nc_ + 1),
                           1024 * nqb:1024 * (nqb + 1)],
                  waits=[otb_free[(g + 1) % 2], ot_tok[g]])
            ot_tok[g + 1] = P.mark("dy")
        P.dma("dy", rbuf[0:1, g % 2, :, :], rsc4[2 * c:2 * c + 2, qb4, :],
              waits=[rb_used[g % 2]] if rb_used[g % 2][1] > 0 else [t_rall])
        t_rl = P.mark("dy")
        t_rb = rb_build(P, pRb, rbuf[0:1, g % 2, :, :],
                        [Rb_free, t_rl])
        t_t = P.op("vector",
                   lambda e, par=par: e.tensor_tensor(
                       tst[:, :], otb[:, par, :], pRb[:, :], ALU.mult),
                   waits=[t_rb, ot_tok[g], tst_free], inc="ve")
        Rb_free = t_t
        otb_free[par] = t_t
        rb_used[g % 2] = t_rb
        t_sq = P.op("scalar",
                    lambda e: e.activation(out=tsq[:, :], in_=tst[:, :],
                                           func=AF.Square),
                    waits=[t_t, tsq_free], inc="sc")
        t_r1 = P.op("vector",
                    lambda e: e.reduce_sum(smal[:, 30:31], tst[:, :],
                                           axis=AXX),
                    waits=[t_t], inc="ve")
        tst_free = t_r1
        t_r2 = P.op("vector",
                    lambda e: e.reduce_sum(smal[:, 31:32], tsq[:, :],
                                           axis=AXX),
                    waits=[t_sq], inc="ve")
        tsq_free = t_r2
        P.op("vector",
             lambda e, c=c: e.tensor_tensor(
                 stat[:, c:c + 1], stat[:, c:c + 1], smal[:, 30:31],
                 ALU.add),
             waits=[t_r1, t_acc], inc="ve")
        t_acc = P.op("vector",
                     lambda e, c=c: e.tensor_tensor(
                         stat[:, 8 + c:9 + c], stat[:, 8 + c:9 + c],
                         smal[:, 31:32], ALU.add),
                     waits=[t_r2, t_acc], inc="ve")
    t_st = P.op("vector", lambda e: e.tensor_copy(smal[:, 32:48], stat[:, :]),
                waits=[t_acc], inc="ve")

    def indmm(e):
        e.matmul(pS2[0:1, 0:16], cb["cInd0"][:, :], smal[:, 32:48],
                 start=True, stop=True)
        return e.matmul(pS2[0:1, 16:32], cb["cInd1"][:, :], smal[:, 32:48],
                        start=True, stop=True)
    t_ind = P.op("tensor", indmm, waits=[t_st, tok_consts], inc="mm")
    sclh = 1.0 / (64.0 * T)
    t_row = P.op("vector",
                 lambda e: e.tensor_scalar(srow[0:1, 0:32], pS2[0:1, 0:32],
                                           sclh, None, ALU.mult),
                 waits=[t_ind], inc="ve")
    t_s1 = P.op("vector",
                lambda e: e.tensor_tensor(srow[0:1, 32:40], srow[0:1, 0:8],
                                          srow[0:1, 0:8], ALU.mult),
                waits=[t_row], inc="ve")
    t_s2 = P.op("vector",
                lambda e: e.tensor_tensor(srow[0:1, 40:48], srow[0:1, 16:24],
                                          srow[0:1, 16:24], ALU.mult),
                waits=[t_row], inc="ve")
    t_s3 = P.op("vector",
                lambda e: e.scalar_tensor_tensor(
                    srow[0:1, 48:56], srow[0:1, 32:40], -1.0,
                    srow[0:1, 8:16], ALU.mult, ALU.add),
                waits=[t_s1], inc="ve")
    t_s4 = P.op("vector",
                lambda e: e.scalar_tensor_tensor(
                    srow[0:1, 56:64], srow[0:1, 40:48], -1.0,
                    srow[0:1, 24:32], ALU.mult, ALU.add),
                waits=[t_s2], inc="ve")
    t_s5 = P.op("scalar",
                lambda e: e.activation(out=srow[0:1, 32:48],
                                       in_=srow[0:1, 48:64], func=AF.Ln,
                                       bias=1e-5, scale=1.0),
                waits=[t_s3, t_s4], inc="sc")
    t_s6 = P.op("scalar",
                lambda e: e.activation(out=srow[0:1, 48:64],
                                       in_=srow[0:1, 32:48], func=AF.Exp,
                                       scale=-0.5),
                waits=[t_s5], inc="sc")
    t_s7 = P.op("vector",
                lambda e: e.scalar_tensor_tensor(
                    srow[0:1, 32:40], srow[0:1, 0:8], -1.0,
                    srow[0:1, 48:56], ALU.mult, ALU.mult),
                waits=[t_s6], inc="ve")
    t_s8 = P.op("vector",
                lambda e: e.scalar_tensor_tensor(
                    srow[0:1, 40:48], srow[0:1, 16:24], -1.0,
                    srow[0:1, 56:64], ALU.mult, ALU.mult),
                waits=[t_s6, t_s7], inc="ve")
    t_s9 = P.op("vector",
                lambda e: e.tensor_copy(srbf[0:1, 0:32], srow[0:1, 32:64]),
                waits=[t_s8], inc="ve")
    t_b1 = P.op("tensor",
                lambda e: e.matmul(pS1[:, :], cb["cOnes"][0:1, :],
                                   srbf[0:1, 16:32], start=True, stop=True),
                waits=[t_s9], inc="mm")
    t_b2_ = P.op("tensor",
                 lambda e: e.matmul(pRb[:, 0:16], cb["cOnes"][0:1, :],
                                    srbf[0:1, 0:16], start=True, stop=True),
                 waits=[t_s9, Rb_free], inc="mm")
    t_w1 = P.op("vector", lambda e: e.tensor_copy(smal[:, 0:16], pS1[:, :]),
                waits=[t_b1], inc="ve")
    t_w2 = P.op("vector",
                lambda e: e.tensor_copy(smal[:, 16:32], pRb[:, 0:16]),
                waits=[t_b2_], inc="ve")
    def gfn(e):
        for c in range(8):
            e.tensor_scalar(gnsc[0:64, c:c + 1], pp["gamma"][0:64, c:c + 1],
                            smal[0:64, c:c + 1], None, ALU.mult)
            e.tensor_scalar(gnsc[64:128, c:c + 1],
                            pp["gamma"][64:128, c:c + 1],
                            smal[64:128, 8 + c:9 + c], None, ALU.mult)
        return e.engine_nop()
    t_g1 = P.op("vector", gfn, waits=[t_w1, tok_consts], inc="ve")

    def gfn2(e):
        for c in range(8):
            e.scalar_tensor_tensor(gnbi[0:64, c:c + 1],
                                   pp["gamma"][0:64, c:c + 1],
                                   smal[0:64, 16 + c:17 + c],
                                   pp["beta"][0:64, c:c + 1], ALU.mult,
                                   ALU.add)
            e.scalar_tensor_tensor(gnbi[64:128, c:c + 1],
                                   pp["gamma"][64:128, c:c + 1],
                                   smal[64:128, 24 + c:25 + c],
                                   pp["beta"][64:128, c:c + 1], ALU.mult,
                                   ALU.add)
        return e.engine_nop()
    P.op("vector", gfn2, waits=[t_g1, t_w2], inc="ve")
    cnt = P.emit(nc, sems, "blkB2")
    ge = dict(P.group_end)
    esB2.close()

    P = Prog(cnt, ge)
    esB3 = ExitStack()
    cv = Carver(ktflat, 8 * max(T, 4096))
    wot = cv.take([128, 8, D], BF16)
    otb = cv.take([128, 2, 1024], BF16)
    gtb = cv.take([128, 2, 1024], BF16)
    tst = cv.take([128, 1024], BF16)
    pts = cv.take([128, 8, 1024], BF16)
    xr3 = cv.take([128, 2, 1024], BF16)
    sln = cv.take([128, 1024], F32)
    tsq = cv.take([128, 1024], F32)
    rbuf = cv.take([1, 2, 2, 1024], BF16)
    pRb3 = esB3.enter_context(nc.psum_tensor("pRb3", [128, 1024], F32))
    pOP = [esB3.enter_context(nc.psum_tensor(f"pOP{i}", [128, 1024], F32))
           for i in range(2)]

    tok_ffnB = ("dg", cnt["dg"])
    for dc in range(8):
        P.dma("dc", wot[:, dc, :],
              sc["woT"][128 * dc:128 * (dc + 1), :], waits=[tok_wT])
    t_wot = P.mark("dc")
    t_gn = P.tok("ve")
    Rb_free = Z
    rb_used = [Z, Z]
    stream_free = [Z, Z]
    outproj_done = {}
    xr_free = [Z, Z]
    sln_free = Z
    tsq_free = Z
    x2_tok = {}
    pOP_free = [Z, Z]

    for qb4 in range(NQ4):
        for c in range(8):
            gi = qb4 * 8 + c
            par = gi % 2
            gate = [stream_free[par]] if stream_free[par][1] > 0 else \
                [t_gn]
            P.dma("dy", otb[:, par, :],
                  sc["ot"][128 * c:128 * (c + 1),
                           1024 * qb4:1024 * (qb4 + 1)],
                  waits=gate)
            P.dma("dy", gtb[:, par, :],
                  sc["gt"][128 * c:128 * (c + 1),
                           1024 * qb4:1024 * (qb4 + 1)])
            P.dma("dy", rbuf[0:1, gi % 2, :, :],
                  rsc4[2 * c:2 * c + 2, qb4, :],
                  waits=[rb_used[gi % 2]] if rb_used[gi % 2][1] > 0
                  else [t_gn])
            t_gt = P.mark("dy")
            t_ot = t_gt
            t_rb = rb_build(P, pRb3, rbuf[0:1, gi % 2, :, :],
                            [Rb_free, t_gt])
            t_t = P.op("vector",
                       lambda e, par=par: e.tensor_tensor(
                           tst[:, :], otb[:, par, :], pRb3[:, :], ALU.mult),
                       waits=[t_rb, t_ot, P.tok("ve")], inc="ve")
            Rb_free = t_t
            rb_used[gi % 2] = t_rb
            t_on = P.op("vector",
                        lambda e, c=c: e.tensor_scalar(
                            tst[:, :], tst[:, :], gnsc[:, c:c + 1],
                            gnbi[:, c:c + 1], ALU.mult, ALU.add),
                        waits=[t_t, t_gn], inc="ve")
            t_p = P.op("vector",
                       lambda e, c=c, par=par: e.tensor_tensor(
                           pts[:, c, :], tst[:, :], gtb[:, par, :],
                           ALU.mult),
                       waits=[t_on, t_gt, outproj_done.get(qb4 - 1, Z)],
                       inc="ve")
            stream_free[par] = t_p
        t_pts = P.tok("ve")
        for tcl in range(8):
            tc = 8 * qb4 + tcl
            xpar = tcl % 2
            xg = [xr_free[xpar]] if xr_free[xpar][1] > 0 else [tok_ffnB]
            P.dma("dg", xr3[:, xpar, :], ap["x"][128 * tc:128 * (tc + 1), :],
                  waits=xg)
            t_xr = P.mark("dg")

            opar2 = tcl % 2

            def opfn(e, tcl=tcl, opar2=opar2):
                for c in range(8):
                    for nh in range(2):
                        e.matmul(pOP[opar2][:, 512 * nh:512 * (nh + 1)],
                                 pts[:, c, 128 * tcl:128 * (tcl + 1)],
                                 wot[:, c, 512 * nh:512 * (nh + 1)],
                                 start=(c == 0), stop=False)
                last = None
                for nh in range(2):
                    last = e.matmul(pOP[opar2][:, 512 * nh:512 * (nh + 1)],
                                    cb["cOnes"][0:1, :],
                                    bo_row[0:1, 512 * nh:512 * (nh + 1)],
                                    start=False, stop=True)
                return last
            t_op = P.op("tensor", opfn, waits=[t_pts, t_wot,
                                               pOP_free[opar2],
                                               tok_S_dg], inc="mm")
            t_s = P.op("vector",
                       lambda e, xpar=xpar, opar2=opar2:
                       e.scalar_tensor_tensor(
                           sln[:, :], pOP[opar2][:, :], 1.0,
                           xr3[:, xpar, :], ALU.bypass, ALU.add),
                       waits=[t_op, t_xr, sln_free], inc="ve")
            pOP_free[opar2] = t_s
            xr_free[xpar] = t_s
            t_sq = P.op("scalar",
                        lambda e: e.activation(out=tsq[:, :], in_=sln[:, :],
                                               func=AF.Square),
                        waits=[t_s, tsq_free], inc="sc")
            t_r1 = P.op("vector",
                        lambda e: e.reduce_sum(smal[:, 40:41], sln[:, :],
                                               axis=AXX),
                        waits=[t_s], inc="ve")
            t_r2 = P.op("vector",
                        lambda e: e.reduce_sum(smal[:, 41:42], tsq[:, :],
                                               axis=AXX),
                        waits=[t_sq], inc="ve")
            tsq_free = t_r2
            t_m = P.op("vector",
                       lambda e: e.tensor_scalar(smal[:, 42:44],
                                                 smal[:, 40:42], 1.0 / D,
                                                 None, ALU.mult),
                       waits=[t_r1, t_r2], inc="ve")
            t_m2 = P.op("vector",
                        lambda e: e.scalar_tensor_tensor(
                            smal[:, 44:45], smal[:, 42:43], smal[:, 42:43],
                            smal[:, 43:44], ALU.mult, ALU.subtract),
                        waits=[t_m], inc="ve")
            t_ln = P.op("scalar",
                        lambda e: e.activation(out=smal[:, 45:46],
                                               in_=smal[:, 44:45],
                                               func=AF.Ln, bias=1e-5,
                                               scale=-1.0),
                        waits=[t_m2], inc="sc")
            t_rs = P.op("scalar",
                        lambda e: e.activation(out=smal[:, 46:47],
                                               in_=smal[:, 45:46],
                                               func=AF.Exp, scale=-0.5),
                        waits=[t_ln], inc="sc")
            t_nm = P.op("vector",
                        lambda e: e.tensor_scalar(
                            smal[:, 47:48], smal[:, 42:43], smal[:, 46:47],
                            -1.0, ALU.mult, ALU.mult),
                        waits=[t_rs], inc="ve")
            x2_tok[tc] = P.op("vector",
                              lambda e, tc=tc: e.tensor_scalar(
                                  x2v[:, tc, :], sln[:, :], smal[:, 46:47],
                                  smal[:, 47:48], ALU.mult, ALU.add),
                              waits=[t_nm], inc="ve")
            sln_free = x2_tok[tc]
        outproj_done[qb4] = P.tok("mm")
    cnt = P.emit(nc, sems, "blkB3")
    ge = dict(P.group_end)
    esB3.close()

    P = Prog(cnt, ge)
    esC = ExitStack()
    cv = Carver(ktflat, 8 * max(T, 4096))
    maskT = cv.take([32, T], BF16)
    eidrow = cv.take([1, T], BF16)
    xt2 = cv.take([128, 8, 256], BF16)
    eidb = cv.take([128, 256], BF16)
    cvv = Carver(vp[:, :, :, :].rearrange("p a b c -> p (a b c)"),
                 (max(4096, 4096) // 128) * H * 65)
    w1e = cvv.take([128, 4, 8, HF], BF16)
    w2e = cvv.take([128, 4, 2, D], BF16)
    hts = cv.take([128, 2, 512], BF16)
    htm = cv.take([128, 2, 512], BF16)
    rwt = cv.take([128, 8, NE], BF16)
    lg = cv.take([128, NE], F32)
    mkb = cv.take([128, NE], BF16)
    sln = cv.take([128, 1024], F32)
    tsq = cv.take([128, 1024], F32)
    oout = cv.take([128, 2, 1024], F32)
    b2sb = cv.take([NE, D], BF16)

    pz = [esC.enter_context(nc.psum_tensor(f"pz{i}", [128, 1024], F32))
          for i in range(2)]
    ph = [esC.enter_context(nc.psum_tensor(f"ph{i}", [128, 512], F32))
          for i in range(2)]
    pmb = esC.enter_context(nc.psum_tensor("pmb", [128, 256], BF16))
    pmf = esC.enter_context(nc.psum_tensor("pmf", [128, 256], F32))

    for dc in range(8):
        P.dma("dc", rwt[:, dc, :], sc["rw"][:, 128 * dc:128 * (dc + 1)],
              transpose=True, waits=[tok_wstage])
    t_rwt = P.mark("dc")
    P.dma("dg", b2sb[:, :], ap["b2"][:, :])
    t_b2s = P.mark("dg")
    we_tok = {}
    w1_used = [Z, Z, Z, Z]
    w2_used = [Z, Z, Z, Z]
    tok_ffnC = ("dg", cnt["dg"])

    def load_expert(P, eL):
        slot = eL % 4
        ne = eL % NE
        s1 = "wa" if slot % 2 == 0 else "wb"
        s2 = "wc" if slot % 2 == 0 else "wd2"
        wg_ = [w1_used[slot], w2_used[slot]]
        if all(t[1] == 0 for t in wg_):
            wg_ = [tok_ffnC]
        P.dma(s1, w1e[:, slot, :, :],
              sc["w1"][ne, :, :].rearrange("(p c) f -> p c f", c=8),
              waits=wg_)
        ta_ = P.mark(s1)
        P.dma(s2, w2e[:, slot, :, :],
              sc["w2"][ne, :, :].rearrange("(p c) f -> p c f", c=2),
              waits=wg_)
        we_tok[eL] = [ta_, P.mark(s2)]

    load_expert(P, 0)
    load_expert(P, 1)

    pmb_free = Z
    pmf_free = Z
    xt2_free = Z
    hts_free = [Z, Z]
    htm_free = [[Z], [Z]]
    ph_free = [Z, Z]
    pz_free = [Z, Z]
    sln_free = Z
    tsq_free = Z
    oout_free = [Z, Z]
    z_done = {}
    st_count = 0
    eL = 0

    for tg in range(NTG):
        tcs = [2 * tg, 2 * tg + 1]
        ev_prev = pmb_free
        for i in range(16):
            tcl, dc = i // 8, i % 8
            t_tr = P.op("tensor",
                        lambda e, tcl=tcl, dc=dc, tcs=tcs: e.transpose(
                            pmb[:, 0:128],
                            x2v[:, tcs[tcl], 128 * dc:128 * (dc + 1)],
                            cb["cIdent"][:, :]),
                        waits=[x2_tok.get(tcs[tcl], Z), xt2_free, ev_prev],
                        inc="mm")
            ev_prev = P.op(
                "vector",
                lambda e, tcl=tcl, dc=dc: e.tensor_copy(
                    xt2[:, dc, 128 * tcl:128 * (tcl + 1)], pmb[:, 0:128]),
                waits=[t_tr], inc="ve")
        t_xt2 = P.tok("ve")
        pmb_free = ev_prev
        t_mts = []
        for tcl in range(2):
            tc = tcs[tcl]

            def rmm(e, tcl=tcl):
                for dc in range(8):
                    e.matmul(pmf[:, 0:NE],
                             xt2[:, dc, 128 * tcl:128 * (tcl + 1)],
                             rwt[:, dc, :], start=(dc == 0), stop=False)
                return e.matmul(pmf[:, 0:NE], cb["cOnes"][0:1, :],
                                rb_row[0:1, :], start=False, stop=True)
            t_rmm = P.op("tensor", rmm,
                         waits=[t_xt2, t_rwt, pmf_free, tok_S_dg], inc="mm")
            t_lg = P.op("vector",
                        lambda e: e.tensor_copy(lg[:, :], pmf[:, 0:NE]),
                        waits=[t_rmm], inc="ve")
            pmf_free = t_lg
            t_mx = P.op("vector",
                        lambda e: e.reduce_max(smal[:, 50:51], lg[:, :],
                                               axis=AXX),
                        waits=[t_lg], inc="ve")
            t_mk = P.op("vector",
                        lambda e: e.tensor_scalar(mkb[:, :], lg[:, :],
                                                  smal[:, 50:51], None,
                                                  ALU.is_equal),
                        waits=[t_mx], inc="ve")
            t_mt = P.op("tensor",
                        lambda e: e.transpose(pmb[0:32, 0:128], mkb[:, :],
                                              cb["cIdent"][:, :]),
                        waits=[t_mk, pmb_free], inc="mm")
            t_mte = P.op("vector",
                         lambda e, tc=tc: e.tensor_copy(
                             maskT[:, 128 * tc:128 * (tc + 1)],
                             pmb[0:32, 0:128]),
                         waits=[t_mt], inc="ve")
            pmb_free = t_mte
            t_mts.append(t_mte)
            t_eid = P.op("tensor",
                         lambda e, tc=tc: e.matmul(
                             pmf[0:1, 0:128], cb["cIotaE"][:, :],
                             maskT[:, 128 * tc:128 * (tc + 1)],
                             start=True, stop=True),
                         waits=[t_mte, pmf_free, tok_consts], inc="mm")
            t_eide = P.op("vector",
                          lambda e, tc=tc: e.tensor_copy(
                              eidrow[0:1, 128 * tc:128 * (tc + 1)],
                              pmf[0:1, 0:128]),
                          waits=[t_eid], inc="ve")
            pmf_free = t_eide
        t_ebm = P.op("tensor",
                     lambda e, tg=tg: e.matmul(
                         pmf[:, 0:256], cb["cOnes"][0:1, :],
                         eidrow[0:1, 256 * tg:256 * (tg + 1)],
                         start=True, stop=True),
                     waits=[pmf_free, P.tok("ve")], inc="mm")
        t_eb = P.op("vector",
                    lambda e: e.tensor_copy(eidb[:, :], pmf[:, 0:256]),
                    waits=[t_ebm, P.tok("ve")], inc="ve")
        pmf_free = t_eb
        w2_pend = []
        for e_ in range(NE):
            epar = eL % 2
            slot = eL % 4
            def w1mm(e, epar=epar, slot=slot):
                last = None
                for hh in range(2):
                    for dc in range(8):
                        last = e.matmul(
                            ph[epar][:, 256 * hh:256 * (hh + 1)],
                            w1e[:, slot, dc, 128 * hh:128 * (hh + 1)],
                            xt2[:, dc, :],
                            start=(dc == 0), stop=(dc == 7))
                return last
            t_w1 = P.op("tensor", w1mm,
                        waits=we_tok[eL] + [t_xt2, ph_free[epar]],
                        inc="mm")
            w1_used[slot] = t_w1
            ee = e_

            def ev1(e, epar=epar, ee=ee):
                e.activation(out=hts[:, epar, 0:256], in_=ph[epar][:, 0:256],
                             func=AF.Relu, bias=b1_pp[:, ee, 0:1], scale=1.0)
                return e.activation(out=hts[:, epar, 256:512],
                                    in_=ph[epar][:, 256:512], func=AF.Relu,
                                    bias=b1_pp[:, ee, 1:2], scale=1.0)
            t_h = P.op("scalar", ev1,
                       waits=[t_w1, hts_free[epar], tok_consts], inc="sc")
            ph_free[epar] = t_h

            def mk2(e, epar=epar, ee=ee):
                e.scalar_tensor_tensor(htm[:, epar, 0:256], eidb[:, :],
                                       float(ee), hts[:, epar, 0:256],
                                       ALU.is_equal, ALU.mult)
                return e.scalar_tensor_tensor(htm[:, epar, 256:512],
                                              eidb[:, :], float(ee),
                                              hts[:, epar, 256:512],
                                              ALU.is_equal, ALU.mult)
            t_hm = P.op("vector", mk2,
                        waits=[t_h, t_eb] + htm_free[epar], inc="ve")
            hts_free[epar] = t_hm

            def mk_w2(epar, slot2, e_, t_hm):
                def w2mm(e, epar=epar, slot2=slot2, e_=e_):
                    last = None
                    for tcl in range(2):
                        for kc in range(2):
                            for nh in range(2):
                                last = e.matmul(
                                    pz[tcl][:, 512 * nh:512 * (nh + 1)],
                                    htm[:, epar,
                                        256 * kc + 128 * tcl:
                                        256 * kc + 128 * (tcl + 1)],
                                    w2e[:, slot2, kc,
                                        512 * nh:512 * (nh + 1)],
                                    start=(e_ == 0 and kc == 0), stop=False)
                    return last
                zw = [t_hm]
                if e_ == 0:
                    zw += [pz_free[0], pz_free[1]]
                t_z = P.op("tensor", w2mm, waits=zw, inc="mm")
                htm_free[epar] = [t_z]
                w2_used[slot2] = t_z
                return t_z
            w2_pend.append((epar, slot, e_, t_hm))
            if len(w2_pend) > 1:
                t_z = mk_w2(*w2_pend.pop(0))
            if eL + 2 < NTG * NE:
                load_expert(P, eL + 2)
            if e_ == NE - 1:
                while w2_pend:
                    t_z = mk_w2(*w2_pend.pop(0))
                def b2mm(e, tg=tg):
                    last = None
                    for tcl in range(2):
                        tc = 2 * tg + tcl
                        for nh in range(2):
                            last = e.matmul(
                                pz[tcl][:, 512 * nh:512 * (nh + 1)],
                                maskT[:, 128 * tc:128 * (tc + 1)],
                                b2sb[:, 512 * nh:512 * (nh + 1)],
                                start=False, stop=True)
                    return last
                t_z = P.op("tensor", b2mm,
                           waits=[t_z, t_b2s] + t_mts, inc="mm")
            eL += 1
        z_done[tg] = t_z
        for tcl in range(2):
            tc = 2 * tg + tcl
            t_s = P.op("vector",
                       lambda e, tcl=tcl, tc=tc: e.scalar_tensor_tensor(
                           sln[:, :], pz[tcl][:, :], 0.0, x2v[:, tc, :],
                           ALU.max, ALU.add),
                       waits=[z_done[tg], sln_free], inc="ve")
            pz_free[tcl] = t_s
            t_sq = P.op("scalar",
                        lambda e: e.activation(out=tsq[:, :], in_=sln[:, :],
                                               func=AF.Square),
                        waits=[t_s, tsq_free], inc="sc")
            t_r1 = P.op("vector",
                        lambda e: e.reduce_sum(smal[:, 40:41], sln[:, :],
                                               axis=AXX),
                        waits=[t_s], inc="ve")
            t_r2 = P.op("vector",
                        lambda e: e.reduce_sum(smal[:, 41:42], tsq[:, :],
                                               axis=AXX),
                        waits=[t_sq], inc="ve")
            tsq_free = t_r2
            t_m = P.op("vector",
                       lambda e: e.tensor_scalar(smal[:, 42:44],
                                                 smal[:, 40:42], 1.0 / D,
                                                 None, ALU.mult),
                       waits=[t_r1, t_r2], inc="ve")
            t_m2 = P.op("vector",
                        lambda e: e.scalar_tensor_tensor(
                            smal[:, 44:45], smal[:, 42:43], smal[:, 42:43],
                            smal[:, 43:44], ALU.mult, ALU.subtract),
                        waits=[t_m], inc="ve")
            t_ln = P.op("scalar",
                        lambda e: e.activation(out=smal[:, 45:46],
                                               in_=smal[:, 44:45],
                                               func=AF.Ln, bias=1e-5,
                                               scale=-1.0),
                        waits=[t_m2], inc="sc")
            t_rs = P.op("scalar",
                        lambda e: e.activation(out=smal[:, 46:47],
                                               in_=smal[:, 45:46],
                                               func=AF.Exp, scale=-0.5),
                        waits=[t_ln], inc="sc")
            t_nm = P.op("vector",
                        lambda e: e.tensor_scalar(
                            smal[:, 47:48], smal[:, 42:43], smal[:, 46:47],
                            -1.0, ALU.mult, ALU.mult),
                        waits=[t_rs], inc="ve")
            t_o = P.op("vector",
                       lambda e, tcl=tcl: e.tensor_scalar(
                           oout[:, tcl, :], sln[:, :], smal[:, 46:47],
                           smal[:, 47:48], ALU.mult, ALU.add),
                       waits=[t_nm, oout_free[tcl]], inc="ve")
            sln_free = t_o
            P.dma("st", out[128 * tc:128 * (tc + 1), :], oout[:, tcl, :],
                  waits=[t_o])
            st_count += 16
            oout_free[tcl] = P.mark("st")
        xt2_free = P.tok("mm")
    P.op("sync", lambda e: e.nop(), waits=[("st", st_count)])
    cnt = P.emit(nc, sems, "blkC")
    esC.close()
    es.close()
    return nc


_NC_CACHE = {}


def _get_program(T):
    if T not in _NC_CACHE:
        _NC_CACHE[T] = build_program(T)
    return _NC_CACHE[T]


def make_in_maps(inputs, T=4096):
    import ml_dtypes
    x = np.ascontiguousarray(np.asarray(inputs["L0_attn_x"],
                                        dtype=np.float32))
    consts = host_constants(T)
    base = {
        "wq": inputs["L0_attn_wq"], "bq": inputs["L0_attn_bq"],
        "wk": inputs["L0_attn_wk"], "bk": inputs["L0_attn_bk"],
        "wv": inputs["L0_attn_wv"], "bv": inputs["L0_attn_bv"],
        "wg": inputs["L0_attn_wg"], "bg": inputs["L0_attn_bg"],
        "wo": inputs["L0_attn_wo"], "bo": inputs["L0_attn_bo"],
        "wd": inputs["L0_attn_wd"], "bd": inputs["L0_attn_bd"],
        "gamma": inputs["L0_attn_gamma"], "beta": inputs["L0_attn_beta"],
        "rw": inputs["L0_ffn_router_w"], "rb": inputs["L0_ffn_router_b"],
        "w1": inputs["L0_ffn_weights1"], "b1": inputs["L0_ffn_bias1"],
        "w2": inputs["L0_ffn_weights2"], "b2": inputs["L0_ffn_bias2"],
    }
    base = {k: np.ascontiguousarray(np.asarray(v, dtype=np.float32))
            for k, v in base.items()}
    for k, v in consts.items():
        dt = CONST_SPECS[k][1]
        base[k] = (np.ascontiguousarray(v.astype(ml_dtypes.bfloat16))
                   if dt == BF16 else np.ascontiguousarray(
                       v.astype(np.float32)))
    ncore = x.shape[0]
    return [dict(base, x=np.ascontiguousarray(x[i])) for i in range(ncore)]


def kernel(**inputs):
    T = 4096
    nc = _get_program(T)
    in_maps = make_in_maps(inputs, T)
    res = run_bass_kernel_spmd(nc, in_maps, core_ids=list(range(NCORES)))
    return np.stack([np.asarray(r["out"], dtype=np.float32)
                     for r in res.results], axis=0)


# version 10
# speedup vs baseline: 1.1305x; 1.1305x over previous
import math
import numpy as np

import concourse.bass as bass
import concourse.mybir as mybir
from concourse.bass_utils import run_bass_kernel_spmd
from concourse.bass import IndirectOffsetOnAxis

F32 = mybir.dt.float32
BF16 = mybir.dt.bfloat16
AF = mybir.ActivationFunctionType
ALU = mybir.AluOpType
AXX = mybir.AxisListType.X

D = 1024
H = 16
HD = 64
NE = 32
HF = 256
CAP = 256
NCORES = 8
LN2C = float(math.log(2.0))

ENGINES = ("tensor", "scalar", "vector", "gpsimd", "sync")
SEMS = ["mm", "sc", "ve", "dg", "dy", "dc", "dw", "st", "wa", "wb", "wc", "wd2"]
QUEUE_OF = {"dg": "gpsimd", "dy": "sync", "dc": "scalar", "dw": "scalar", "st": "sync",
            "wa": "sync", "wb": "sync", "wc": "gpsimd", "wd2": "gpsimd"}
Z = ("ve", 0)


def host_constants(T):
    c = {}
    c["cIdent"] = np.eye(128, dtype=np.float32)
    P = np.zeros((128, 128), dtype=np.float32)
    for k in range(64):
        P[2 * k, 2 * k + 1] = -1.0
        P[2 * k + 1, 2 * k] = 1.0
    c["cPerm"] = np.ascontiguousarray(P.T)
    t64 = np.arange(T, dtype=np.float64)
    j64 = np.arange(0, HD, 2, dtype=np.float64)
    inv_freq = 1.0 / (10000.0 ** (j64 / HD))
    ang = t64[:, None] * inv_freq[None, :]
    cos64 = np.cos(np.concatenate([ang, ang], axis=1))
    sin64 = np.sin(np.concatenate([ang, ang], axis=1))
    c["cCos"] = np.tile(cos64.T, (2, 1)).astype(np.float32)
    c["cSin"] = np.tile(sin64.T, (2, 1)).astype(np.float32)
    c["cJ"] = np.broadcast_to(np.arange(1024, dtype=np.float32),
                              (128, 1024)).copy()
    p = np.arange(128, dtype=np.float32)
    c["cPCM"] = (128.0 * np.arange(32, dtype=np.float32)[None, :]
                 + p[:, None]).astype(np.float32)
    c["cOnes"] = np.ones((1, 128), dtype=np.float32)
    c["cLT"] = np.ascontiguousarray(
        np.tril(np.ones((128, 128), dtype=np.float32)).T)
    c["cIdF"] = np.eye(128, dtype=np.float32)
    c["cIotaC"] = (np.arange(32, dtype=np.float32) * CAP)[:, None]
    c["cSc"] = np.broadcast_to(np.array([-5.0 * LN2C, 1e-5],
                                        dtype=np.float32), (128, 2)).copy()
    c["cInd0"] = (p < 64).astype(np.float32)[:, None]
    c["cInd1"] = (p >= 64).astype(np.float32)[:, None]
    c["cIotaE"] = np.arange(NE, dtype=np.float32)[:, None]
    return c


CONST_SPECS = {
    "cIdent": (lambda T: [128, 128], BF16),
    "cPerm": (lambda T: [128, 128], BF16),
    "cCos": (lambda T: [128, T], F32),
    "cSin": (lambda T: [128, T], F32),
    "cJ": (lambda T: [128, 1024], F32),
    "cPCM": (lambda T: [128, 32], F32),
    "cSc": (lambda T: [128, 2], F32),
    "cOnes": (lambda T: [1, 128], BF16),
    "cLT": (lambda T: [128, 128], BF16),
    "cIdF": (lambda T: [128, 128], F32),
    "cIotaC": (lambda T: [32, 1], F32),
    "cInd0": (lambda T: [128, 1], F32),
    "cInd1": (lambda T: [128, 1], F32),
    "cIotaE": (lambda T: [32, 1], BF16),
}
STREAMED_CONSTS = ("cCos", "cSin")


class Prog:

    def __init__(self, cnt=None, group_end=None):
        self.cnt = dict(cnt) if cnt else {s: 0 for s in SEMS}
        self.group_end = dict(group_end) if group_end else {}
        self.ops = {e: [] for e in ENGINES}

    def op(self, eng, fn, waits=(), inc=None, incby=1):
        w = [(s, v) for (s, v) in waits if v > 0]
        self.ops[eng].append((fn, w, inc, incby))
        if inc is not None:
            self.cnt[inc] += incby
        return self.tok(inc) if inc else None

    def dma(self, q, out, in_, transpose=False, waits=(), slow=False):
        w = list(waits) + [(q, self.group_end.get(q, 0))]
        if transpose:
            fn = (lambda e, out=out, in_=in_:
                  e.dma_start(out=out, in_=in_, transpose=True))
        else:
            fn = (lambda e, out=out, in_=in_, slow=slow: e.dma_start(
                out=out, in_=in_, allow_slow_non_contiguous=slow))
        return self.op(QUEUE_OF[q], fn, waits=w, inc=q, incby=16)

    def mark(self, q):
        self.group_end[q] = self.cnt[q]
        return (q, self.cnt[q])

    def tok(self, sem):
        return (sem, self.cnt[sem])

    def emit(self, nc, sems, name):
        with nc.Block(name) as blk:
            for eng_name in ENGINES:
                ops = self.ops[eng_name]
                fin = [(s, self.cnt[s]) for s, q in QUEUE_OF.items()
                       if q == eng_name and self.cnt[s] > 0]
                if not ops and not fin:
                    continue

                def body(e, ops=ops, fin=fin):
                    last_wait = {}
                    for (fn, waits, inc, incby) in ops:
                        for (s, v) in waits:
                            if last_wait.get(s, -1) >= v:
                                continue
                            e.wait_ge(sems[s], v)
                            last_wait[s] = v
                        ins = fn(e)
                        if inc is not None:
                            ins.then_inc(sems[inc], incby)
                    for (s, v) in fin:
                        if last_wait.get(s, -1) < v:
                            e.wait_ge(sems[s], v)
                getattr(blk, eng_name)(body)
        return dict(self.cnt)


class Carver:
    def __init__(self, tensor, nelem):
        self.t = tensor
        self.n = nelem
        self.off = 0

    def take(self, shape, dt):
        n = int(np.prod(shape[1:]))
        if dt in (F32, mybir.dt.int32, mybir.dt.uint32):
            if self.off % 2:
                self.off += 1
            n *= 2
        v = self.t[:shape[0], self.off:self.off + n]
        self.off += n
        assert self.off <= self.n, f"carve overflow {self.off} > {self.n}"
        if dt in (F32, mybir.dt.int32, mybir.dt.uint32):
            v = v.bitcast(dt)
        if len(shape) == 3:
            v = v.rearrange("p (a b) -> p a b", a=shape[1], b=shape[2])
        elif len(shape) == 4:
            v = v.rearrange("p (a b c) -> p a b c", a=shape[1], b=shape[2],
                            c=shape[3])
        return v


def build_program(T=4096, detect_races=True):
    nc = bass.Bass("TRN2", target_bir_lowering=False,
                   detect_race_conditions=detect_races)
    NTB = T // 512
    NQ4 = T // 1024
    NTG = T // 256
    NKT = T // 128

    ap = {}
    ap["x"] = nc.dram_tensor("x", [T, D], F32, kind="ExternalInput").ap()
    for w in ["wq", "wk", "wv", "wg", "wo"]:
        ap[w] = nc.dram_tensor(w, [D, D], F32, kind="ExternalInput").ap()
    for b in ["bq", "bk", "bv", "bg", "bo", "gamma", "beta"]:
        ap[b] = nc.dram_tensor(b, [D], F32, kind="ExternalInput").ap()
    ap["wd"] = nc.dram_tensor("wd", [H, D], F32, kind="ExternalInput").ap()
    ap["bd"] = nc.dram_tensor("bd", [H], F32, kind="ExternalInput").ap()
    ap["rw"] = nc.dram_tensor("rw", [NE, D], F32, kind="ExternalInput").ap()
    ap["rb"] = nc.dram_tensor("rb", [NE], F32, kind="ExternalInput").ap()
    ap["w1"] = nc.dram_tensor("w1", [NE, D, HF], F32, kind="ExternalInput").ap()
    ap["b1"] = nc.dram_tensor("b1", [NE, HF], F32, kind="ExternalInput").ap()
    ap["w2"] = nc.dram_tensor("w2", [NE, HF, D], F32, kind="ExternalInput").ap()
    ap["b2"] = nc.dram_tensor("b2", [NE, D], F32, kind="ExternalInput").ap()
    for name, (shfn, dt) in CONST_SPECS.items():
        ap[name] = nc.dram_tensor(name, shfn(T), dt, kind="ExternalInput").ap()
    out = nc.dram_tensor("out", [T, D], F32, kind="ExternalOutput").ap()

    sc = {}
    for w in ["wq", "wk", "wv", "wg", "wo"]:
        sc[w] = nc.dram_tensor(w + "_bf", [D, D], BF16, kind="Internal").ap()
    sc["rw"] = nc.dram_tensor("rw_bf", [NE, D], BF16, kind="Internal").ap()
    sc["wd"] = nc.dram_tensor("wd_bf", [H, D], BF16, kind="Internal").ap()
    sc["w1"] = nc.dram_tensor("w1_bf", [NE, D, HF], BF16, kind="Internal").ap()
    sc["w2"] = nc.dram_tensor("w2_bf", [NE, HF, D], BF16, kind="Internal").ap()
    sc["qt"] = nc.dram_tensor("qt_sc", [D, T], BF16, kind="Internal").ap()
    sc["gt"] = nc.dram_tensor("gt_sc", [D, T], BF16, kind="Internal").ap()
    sc["ot"] = nc.dram_tensor("ot_sc", [D, T], BF16, kind="Internal").ap()
    sc["dd"] = nc.dram_tensor("d_sc", [H * NQ4, 1024], BF16,
                              kind="Internal").ap()
    sc["rr"] = nc.dram_tensor("r_sc", [H * NQ4, 1024], BF16,
                              kind="Internal").ap()
    for w in ["wq", "wk", "wv", "wg", "wo"]:
        sc[w + "T"] = nc.dram_tensor(w + "T_bf", [D, D], BF16,
                                     kind="Internal").ap()
    sc["xs"] = nc.dram_tensor("xs_sc", [NE * CAP, D], BF16,
                              kind="Internal").ap()
    sc["ys"] = nc.dram_tensor("ys_sc", [NE * CAP, D], BF16,
                              kind="Internal").ap()
    sc["b2"] = nc.dram_tensor("b2_bf", [NE, D], BF16, kind="Internal").ap()

    from contextlib import ExitStack
    es = ExitStack()

    def sb(name, shape, dt):
        return es.enter_context(nc.sbuf_tensor(name, shape, dt))

    sems = {s: es.enter_context(nc.semaphore("sem_" + s)) for s in SEMS}

    big = sb("big", [128, 64 * 512], BF16)
    ktb = sb("ktb", [128, 8, max(T, 4096)], BF16)
    vp = sb("vp", [128, max(T, 4096) // 128, H, 65], BF16)
    cb = {}
    for name, (shfn, dt) in CONST_SPECS.items():
        if name in STREAMED_CONSTS:
            continue
        cb[name] = sb(name + "_s", shfn(T), dt)
    pp = {}
    for b in ["bq", "bk", "bg", "gamma", "beta"]:
        pp[b] = sb(b + "_pp", [128, 8], F32)
    bo_row = sb("bo_row", [1, D], BF16)
    rb_row = sb("rb_row", [1, NE], BF16)
    bd_pp = sb("bd_pp", [16, 1], F32)
    b1_pp = sb("b1_pp", [128, NE, 2], F32)
    ldB = sb("ldB", [128, 16], F32)
    rall = sb("rall", [H * (T // 1024), 1024], BF16)
    gnsc = sb("gnsc", [128, 8], F32)
    gnbi = sb("gnbi", [128, 8], F32)
    xm = sb("xm", [128, 8], F32)
    stat = sb("stat", [128, 16], F32)
    smal = sb("smal", [128, 64], F32)
    smbf = sb("smbf", [128, 64], BF16)
    srow = sb("srow", [1, 64], F32)
    srbf = sb("srbf", [1, 64], BF16)

    x2v = big[:, :].rearrange("p (tc d) -> p tc d", d=D)
    ktflat = ktb[:, :, :].rearrange("p a b -> p (a b)")
    nc.const_aps.register(F32, -5.0 * LN2C, cb["cSc"][:, 0:1])
    nc.const_aps.register(F32, 1e-5, cb["cSc"][:, 1:2])

    P = Prog()
    for w in ["wq", "wk", "wv", "wg", "wo"]:
        for hh in range(2):
            P.dma("dg", sc[w][512 * hh:512 * (hh + 1), :],
                  ap[w][512 * hh:512 * (hh + 1), :])
    P.dma("dg", sc["rw"][:, :], ap["rw"][:, :])
    P.dma("dg", sc["b2"][:, :], ap["b2"][:, :])
    P.dma("dg", sc["wd"][:, :], ap["wd"][:, :])
    P.dma("dg", bo_row[0:1, :], ap["bo"].rearrange("(o d) -> o d", o=1))
    P.dma("dg", rb_row[0:1, :], ap["rb"].rearrange("(o d) -> o d", o=1))
    tok_S_dg = P.mark("dg")
    tok_wstage = tok_S_dg
    for name in cb:
        P.dma("dy", cb[name][...], ap[name][...])
    for b in ["bq", "bk", "bg", "gamma", "beta"]:
        P.dma("dy", pp[b][:, :], ap[b].rearrange("(c p) -> p c", p=128),
              slow=True)
    P.dma("dy", bd_pp[:, :], ap["bd"].rearrange("(p o) -> p o", o=1),
          slow=True)
    P.dma("dy", b1_pp[:, :, :],
          ap["b1"].rearrange("e (h p) -> p e h", p=128), slow=True)
    tok_consts = P.mark("dy")
    cvS = Carver(big, 64 * 512)
    tbuf = cvS.take([128, 2, 1024], BF16)
    tb_free = [Z, Z]
    ti = 0
    tok_wT = {}
    for w in ["wq", "wk", "wv", "wg", "wo"]:
        for dc in range(8):
            par = ti % 2
            P.dma("dc", tbuf[:, par, :],
                  sc[w][:, 128 * dc:128 * (dc + 1)], transpose=True,
                  waits=[tok_wstage, tb_free[par]])
            t_l = P.mark("dc")
            P.dma("dy", sc[w + "T"][128 * dc:128 * (dc + 1), :],
                  tbuf[:, par, :], waits=[t_l])
            tb_free[par] = P.mark("dy")
            ti += 1
        tok_wT[w] = P.tok("dy")
    cnt = P.emit(nc, sems, "blkS")
    ge = dict(P.group_end)

    P = Prog(cnt, ge)
    esA = ExitStack()
    cv = Carver(big, 64 * 512)
    xrow = cv.take([128, 4, D], BF16)
    xt = cv.take([128, 8, 512], BF16)
    wts = cv.take([128, 2, 8, D], BF16)
    qts = cv.take([128, 2, 512], BF16)
    qtr = cv.take([128, 2, 512], BF16)
    rt1 = cv.take([128, 512], BF16)
    rt2 = cv.take([128, 512], BF16)
    cosl = cv.take([128, 2, 512], F32)
    sinl = cv.take([128, 2, 512], F32)
    bv_row = cv.take([1, D], BF16)
    wdt = esA.enter_context(nc.sbuf_tensor("wdt", [128, 8, 16], BF16))

    ptx = [esA.enter_context(nc.psum_tensor(f"ptx{i}", [128, 128], BF16))
           for i in range(2)]
    pq = [esA.enter_context(nc.psum_tensor(f"pq{i}", [128, 512], F32))
          for i in range(2)]
    pr = esA.enter_context(nc.psum_tensor("pr", [128, 512], F32))
    pv = esA.enter_context(nc.psum_tensor("pv", [128, D], F32))

    P.op("vector", lambda e: e.memset(vp[:, :, :, :], 1.0), inc="ve")
    P.op("vector", lambda e: e.memset(xm[:, :], 0.0), inc="ve")
    P.op("vector", lambda e: e.memset(stat[:, :], 0.0), inc="ve")
    P.op("vector", lambda e: e.memset(smbf[:, :], 0.0), inc="ve")
    P.op("vector",
         lambda e: e.tensor_scalar(smal[:, 52:60], pp["bq"][:, :], 0.125,
                                   None, ALU.mult),
         waits=[tok_consts], inc="ve")
    t_init = P.tok("ve")
    P.dma("dg", bv_row[0:1, :], ap["bv"].rearrange("(o d) -> o d", o=1),
          waits=[tok_S_dg])
    t_bv = P.mark("dg")
    for dc in range(8):
        P.dma("dw", wdt[:, dc, :], sc["wd"][:, 128 * dc:128 * (dc + 1)],
              transpose=True, waits=[tok_wstage])
    t_wdt = P.mark("dw")

    WL = ["q", "k", "g", "v"]
    chain_done = {}
    wts_tok = {}
    for ci in range(2):
        gate = [tok_wT["w" + WL[ci]]]
        for dc in range(8):
            P.dma("dc", wts[:, ci % 2, dc, :],
                  sc["w" + WL[ci] + "T"][128 * dc:128 * (dc + 1), :],
                  waits=gate)
        wts_tok[ci] = P.mark("dc")

    evac_tok = {}
    xtr_done = {}
    pq_free = [Z, Z]
    qts_free = [[], []]
    qtr_free = [Z, Z]
    rt_last = Z
    pr_free = Z
    pv_free = Z
    cos_used = {}

    for tb in range(NTB):
        cpar = tb % 2
        xw = [t_init] + ([xtr_done[tb - 1]] if tb >= 1 else [])
        for ts_ in range(4):
            P.dma("dg", xrow[:, ts_, :],
                  ap["x"][512 * tb + 128 * ts_:512 * tb + 128 * (ts_ + 1), :],
                  waits=xw)
        t_x = P.mark("dg")
        cw = [cos_used[tb - 2]] if tb >= 2 else [tok_consts]
        P.dma("dy", cosl[:, cpar, :],
              ap["cCos"][:, 512 * tb:512 * (tb + 1)], waits=cw)
        P.dma("dy", sinl[:, cpar, :],
              ap["cSin"][:, 512 * tb:512 * (tb + 1)])
        t_cs = P.mark("dy")
        for i in range(32):
            dc, ts_ = i % 8, i // 8
            tw = [t_x, tok_consts]
            if (tb, i - 2) in evac_tok:
                tw.append(evac_tok[(tb, i - 2)])
            elif tb > 0:
                tw.append(evac_tok[(tb - 1, 30 + i)])
            tpi = i % 2
            t_mm = P.op("tensor",
                        lambda e, tpi=tpi, ts_=ts_, dc=dc: e.transpose(
                            ptx[tpi][:, :],
                            xrow[:, ts_, 128 * dc:128 * (dc + 1)],
                            cb["cIdent"][:, :]),
                        waits=tw, inc="mm")
            ew = [t_mm]
            if tb > 0:
                ew.append(chain_done[4 * tb - 1])
            evac_tok[(tb, i)] = P.op(
                "vector",
                lambda e, tpi=tpi, dc=dc, ts_=ts_: e.tensor_copy(
                    xt[:, dc, 128 * ts_:128 * (ts_ + 1)], ptx[tpi][:, :]),
                waits=ew, inc="ve")
        xtr_done[tb] = P.tok("mm")
        t_xt = P.tok("ve")
        for dc in range(8):
            P.op("vector",
                 lambda e, dc=dc: e.reduce_sum(smal[:, dc:dc + 1],
                                               xt[:, dc, :], axis=AXX),
                 waits=[t_xt], inc="ve")
        t_red = P.tok("ve")
        P.op("vector",
             lambda e: e.tensor_tensor(xm[:, :], xm[:, :], smal[:, 0:8],
                                       ALU.add),
             waits=[t_red], inc="ve")
        for iw, w in enumerate(WL):
            ci = 4 * tb + iw
            par = ci % 2
            if w in ("q", "k", "g"):
                rot_pend = []

                def do_rot(ec, qpar, t_ev, w=w, tb=tb, cpar=cpar):
                    global_ns = None
                    t_rot = P.op(
                        "tensor",
                        lambda e, qpar=qpar: e.matmul(
                            pr[:, :], cb["cPerm"][:, :], qts[:, qpar, :],
                            start=True, stop=True),
                        waits=[t_ev, rot_state["pr_free"]], inc="mm")
                    t_r1 = P.op(
                        "vector",
                        lambda e, cpar=cpar: e.tensor_tensor(
                            rt1[:, :], pr[:, :],
                            sinl[:, cpar, :], ALU.mult),
                        waits=[t_rot, t_cs, rot_state["rt_last"]], inc="ve")
                    rot_state["pr_free"] = t_r1
                    t_r2 = P.op(
                        "vector",
                        lambda e, qpar=qpar, cpar=cpar:
                        e.tensor_tensor(rt2[:, :], qts[:, qpar, :],
                                        cosl[:, cpar, :], ALU.mult),
                        waits=[t_ev, rot_state["rt_last"]], inc="ve")
                    qts_free[qpar] = [t_r2]
                    if w == "k":
                        rt_l = P.op(
                            "vector",
                            lambda e, ec=ec, tb=tb: e.tensor_tensor(
                                ktb[:, ec, 512 * tb:512 * (tb + 1)],
                                rt1[:, :], rt2[:, :], ALU.add),
                            waits=[t_r1, t_r2], inc="ve")
                    else:
                        dpar = ec % 2
                        rt_l = P.op(
                            "vector",
                            lambda e, dpar=dpar: e.tensor_tensor(
                                qtr[:, dpar, :], rt1[:, :], rt2[:, :],
                                ALU.add),
                            waits=[t_r1, t_r2, qtr_free[dpar]], inc="ve")
                        P.dma(
                            "dy",
                            sc["qt"][128 * ec:128 * (ec + 1),
                                     512 * tb:512 * (tb + 1)],
                            qtr[:, dpar, :], waits=[rt_l])
                        qtr_free[dpar] = P.mark("dy")
                    rot_state["rt_last"] = rt_l

                rot_state = {"pr_free": pr_free, "rt_last": rt_last}
                for ec in range(8):
                    ppar = ec % 2

                    def mmfn(e, par=par, ppar=ppar, ec=ec):
                        last = None
                        for dc in range(8):
                            last = e.matmul(
                                pq[ppar][:, :],
                                wts[:, par, dc, 128 * ec:128 * (ec + 1)],
                                xt[:, dc, :],
                                start=(dc == 0), stop=(dc == 7))
                        return last
                    t_mm = P.op("tensor", mmfn,
                                waits=[wts_tok[ci], t_xt, pq_free[ppar]],
                                inc="mm")
                    func = AF.Silu if w == "g" else AF.Identity
                    if w == "q":
                        bias = smal[:, 52 + ec:53 + ec]
                        scl = 0.125
                    else:
                        bias = pp["b" + w][:, ec:ec + 1]
                        scl = 1.0
                    qpar = ec % 2
                    t_ev = P.op("scalar",
                                lambda e, qpar=qpar, ppar=ppar, func=func,
                                bias=bias, scl=scl: e.activation(
                                    out=qts[:, qpar, :], in_=pq[ppar][:, :],
                                    func=func, bias=bias, scale=scl),
                                waits=[t_mm, tok_consts, t_init]
                                + qts_free[qpar],
                                inc="sc")
                    pq_free[ppar] = t_ev
                    if w == "g":
                        P.dma("dy",
                              sc["gt"][128 * ec:128 * (ec + 1),
                                       512 * tb:512 * (tb + 1)],
                              qts[:, qpar, :], waits=[t_ev])
                        qts_free[qpar] = [P.mark("dy")]
                    else:
                        rot_pend.append((ec, qpar, t_ev))
                        if len(rot_pend) > 1:
                            do_rot(*rot_pend.pop(0))
                while rot_pend:
                    do_rot(*rot_pend.pop(0))
                pr_free = rot_state["pr_free"]
                rt_last = rot_state["rt_last"]
                cos_used[tb] = P.tok("ve")
            else:
                for ts_ in range(4):
                    def vmm(e, par=par, ts_=ts_):
                        for dc in range(8):
                            for nh in range(2):
                                e.matmul(pv[:, 512 * nh:512 * (nh + 1)],
                                         xt[:, dc, 128 * ts_:128 * (ts_ + 1)],
                                         wts[:, par, dc,
                                             512 * nh:512 * (nh + 1)],
                                         start=(dc == 0), stop=False)
                        last = None
                        for nh in range(2):
                            last = e.matmul(pv[:, 512 * nh:512 * (nh + 1)],
                                            cb["cOnes"][0:1, :],
                                            bv_row[0:1,
                                                   512 * nh:512 * (nh + 1)],
                                            start=False, stop=True)
                        return last
                    t_mm = P.op("tensor", vmm,
                                waits=[wts_tok[ci], t_xt, t_bv, pv_free],
                                inc="mm")
                    kt = 4 * tb + ts_
                    pv_free = P.op(
                        "vector",
                        lambda e, kt=kt: e.tensor_copy(
                            vp[:, kt, :, 0:64],
                            pv[:, :].rearrange("p (h d) -> p h d", h=H)),
                        waits=[t_mm, t_init], inc="ve")
            chain_done[ci] = P.tok("mm")
            if ci + 2 < 4 * NTB:
                nci = ci + 2
                nw = WL[nci % 4]
                for dc in range(8):
                    P.dma("dc", wts[:, nci % 2, dc, :],
                          sc["w" + nw + "T"][128 * dc:128 * (dc + 1), :],
                          waits=[chain_done[ci], tok_wT["w" + nw]])
                wts_tok[nci] = P.mark("dc")

    t_a = P.op("vector",
               lambda e: e.tensor_scalar(smbf[:, 0:8], xm[:, :], 1.0 / T,
                                         None, ALU.mult),
               waits=[P.tok("ve")], inc="ve")

    def pdmm(e):
        last = None
        for dc in range(8):
            last = e.matmul(pq[0][0:16, 0:1], wdt[:, dc, :],
                            smbf[:, dc:dc + 1],
                            start=(dc == 0), stop=(dc == 7))
        return last
    t_pd = P.op("tensor", pdmm, waits=[t_a, t_wdt, pq_free[0]], inc="mm")
    t_b = P.op("vector",
               lambda e: e.tensor_tensor(smal[0:16, 10:11], pq[0][0:16, 0:1],
                                         bd_pp[:, :], ALU.add),
               waits=[t_pd, tok_consts], inc="ve")
    t_c = P.op("scalar",
               lambda e: e.activation(out=smal[0:16, 9:10],
                                      in_=smal[0:16, 10:11],
                                      func=AF.Sigmoid),
               waits=[t_b], inc="sc")
    t_d = P.op("scalar",
               lambda e: e.activation(out=smal[0:16, 11:12],
                                      in_=smal[0:16, 9:10], func=AF.Exp,
                                      bias=-5.0 * LN2C, scale=-8.0 * LN2C),
               waits=[t_c], inc="sc")
    t_e = P.op("vector",
               lambda e: e.tensor_scalar(smal[0:16, 13:14],
                                         smal[0:16, 11:12], -1.0, 1.0,
                                         ALU.mult, ALU.add),
               waits=[t_d], inc="ve")
    t_f = P.op("scalar",
               lambda e: e.activation(out=smal[0:16, 12:13],
                                      in_=smal[0:16, 13:14], func=AF.Ln),
               waits=[t_e], inc="sc")
    t_g = P.op("vector",
               lambda e: e.tensor_copy(smbf[0:16, 8:9], smal[0:16, 12:13]),
               waits=[t_f], inc="ve")
    t_h = P.op("tensor",
               lambda e: e.transpose(ptx[0][0:1, 0:128], smbf[:, 8:9],
                                     cb["cIdent"][:, :]),
               waits=[t_g, evac_tok[(NTB - 1, 31)]], inc="mm")
    t_i = P.op("vector",
               lambda e: e.tensor_copy(srbf[0:1, 0:16], ptx[0][0:1, 0:16]),
               waits=[t_h], inc="ve")
    t_j = P.op("tensor",
               lambda e: e.matmul(pq[1][:, 0:16], cb["cOnes"][0:1, :],
                                  srbf[0:1, 0:16], start=True, stop=True),
               waits=[t_i, pq_free[1]], inc="mm")
    P.op("vector", lambda e: e.tensor_copy(ldB[:, :], pq[1][:, 0:16]),
         waits=[t_j], inc="ve")
    cnt = P.emit(nc, sems, "blkA")
    ge = dict(P.group_end)
    esA.close()

    P = Prog(cnt, ge)
    esB = ExitStack()
    cv = Carver(big, 64 * 512)
    qt_sb = cv.take([128, 2, 1024], BF16)
    ptraw = cv.take([128, 2, 1024], BF16)
    ptfin = cv.take([128, 4, 1024], BF16)
    hBt = cv.take([128, 1024], BF16)
    otm = cv.take([128, 2, 1024], BF16)
    ut = cv.take([128, 2, 1024], F32)
    lJ = cv.take([128, 1024], F32)
    tm = cv.take([128, 1024], F32)
    cJl = cv.take([128, 1024], F32)
    bvD = cv.take([128, 32], F32)
    dvq = cv.take([128, 2, 32], F32)
    dall = cv.take([H * NQ4, 1024], BF16)
    lnd = cv.take([H * NQ4, 1024], F32)
    ztile = cv.take([128, 1024], BF16)

    Sp = [esB.enter_context(nc.psum_tensor(f"S{i}", [128, 1024], F32))
          for i in range(2)]
    pOT = [esB.enter_context(nc.psum_tensor(f"pOT{i}", [65, 1024], F32))
           for i in range(2)]

    P.dma("dy", cJl[:, :], ap["cJ"][:, :], waits=[tok_consts])
    tok_cj = P.mark("dy")
    for e_ in range(NE):
        for cc in range(8):
            P.dma("dg",
                  sc["w1"][e_, :, :].rearrange("(p c) f -> c p f", c=8)[cc],
                  ap["w1"][e_, 128 * cc:128 * (cc + 1), :], slow=True)
        for cc in range(2):
            P.dma("dg",
                  sc["w2"][e_, :, :].rearrange("(p c) f -> c p f", c=2)[cc],
                  ap["w2"][e_, 128 * cc:128 * (cc + 1), :], slow=True)
    t_zt = P.op("vector", lambda e: e.memset(ztile[:, :], 0.0), inc="ve")
    for blk_i in range(NE * CAP // 128):
        P.dma("dg", sc["xs"][128 * blk_i:128 * (blk_i + 1), :],
              ztile[:, :], waits=[t_zt])
    tok_ffn = P.mark("dg")

    t_ldB = P.tok("ve")
    qt_tok = {}
    P.dma("dy", qt_sb[:, 0, :], sc["qt"][0:128, 0:1024])
    qt_tok[0] = P.mark("dy")

    S_free = [Z, Z]
    pt_free = [Z, Z, Z, Z]
    praw_free = [Z, Z]
    ut_free = [Z, Z]
    otm_free = [Z, Z]
    pOT_free = [Z, Z]
    tm_free = Z
    dvq_free = [[], []]
    prep_free = {"bvD": Z, "lJ": [], "hB": Z}
    scores_done = {0: Z, 1: Z}
    h_prep = {}

    NG = 8 * NQ4
    for g in range(NG):
        pr_, qb4 = divmod(g, NQ4)
        gpar = g % 2
        if g + 1 < NG:
            npr, nqb = divmod(g + 1, NQ4)
            P.dma("dy", qt_sb[:, (g + 1) % 2, :],
                  sc["qt"][128 * npr:128 * (npr + 1),
                           1024 * nqb:1024 * (nqb + 1)],
                  waits=[scores_done[(g + 1) % 2], qt_tok[g]])
            qt_tok[g + 1] = P.mark("dy")
        for sub in range(2):
            h = 2 * pr_ + sub
            if qb4 == 0:
                t_bvd = P.op("vector",
                             lambda e, h=h: e.tensor_scalar(
                                 bvD[:, :], cb["cPCM"][:, :],
                                 ldB[:, h:h + 1], None, ALU.mult),
                             waits=[prep_free["bvD"], tok_consts, t_ldB],
                             inc="ve")
                t_lj = P.op("vector",
                            lambda e, h=h: e.tensor_scalar(
                                lJ[:, :], cJl[:, :], ldB[:, h:h + 1], None,
                                ALU.mult),
                            waits=[tok_cj, t_ldB] + prep_free["lJ"],
                            inc="ve")
                t_hb = P.op("scalar",
                            lambda e: e.activation(out=hBt[:, :],
                                                   in_=lJ[:, :],
                                                   func=AF.Exp),
                            waits=[t_lj, prep_free["hB"]], inc="sc")
                h_prep[h] = (t_bvd, t_lj, t_hb)
            t_bvd, t_lj, t_hb = h_prep[h]
            dpar = (2 * g + sub) % 2
            t_q1 = P.op("vector",
                        lambda e, h=h, qb4=qb4: e.tensor_scalar(
                            smal[:, 20:21], ldB[:, h:h + 1],
                            -1024.0 * qb4, None, ALU.mult),
                        waits=[t_bvd], inc="ve")
            t_dvq = P.op("vector",
                         lambda e, dpar=dpar: e.tensor_scalar(
                             dvq[:, dpar, :], bvD[:, :], smal[:, 20:21],
                             None, ALU.add),
                         waits=[t_q1] + dvq_free[dpar], inc="ve")
            r0, r1 = (0, 64) if sub == 0 else (64, 128)
            tp = (0, 0) if sub == 0 else (64, 0)
            av_pend = []

            def emit_av(kt, par4, t_pt):
                def avfn(e, kt=kt, par4=par4, h=h):
                    last = None
                    for qh in range(2):
                        e.matmul(pOT[0][:, 512 * qh:512 * (qh + 1)],
                                 vp[0:64, kt, h, :],
                                 ptfin[0:64, par4, 512 * qh:512 * (qh + 1)],
                                 start=(kt == 0), stop=(kt == NKT - 1),
                                 tile_position=(0, 0))
                        last = e.matmul(
                            pOT[1][:, 512 * qh:512 * (qh + 1)],
                            vp[64:128, kt, h, :],
                            ptfin[64:128, par4, 512 * qh:512 * (qh + 1)],
                            start=(kt == 0), stop=(kt == NKT - 1),
                            tile_position=(64, 0))
                    return last
                avw = [t_pt]
                if kt == 0:
                    avw += pOT_free
                pt_free[par4] = P.op("tensor", avfn, waits=avw, inc="mm")

            for kt in range(NKT):
                par = kt % 2
                par4 = kt % 4
                ms = kt - 8 * qb4
                def scfn(e, pr_=pr_, kt=kt, par=par, tp=tp, r0=r0, r1=r1,
                         gpar=gpar):
                    last = None
                    for qh in range(2):
                        last = e.matmul(
                            Sp[par][:, 512 * qh:512 * (qh + 1)],
                            ktb[r0:r1, pr_, 128 * kt:128 * (kt + 1)],
                            qt_sb[r0:r1, gpar, 512 * qh:512 * (qh + 1)],
                            start=True, stop=True, tile_position=tp)
                    return last
                t_mm = P.op("tensor", scfn,
                            waits=[qt_tok[g], S_free[par]], inc="mm")
                if ms >= 8:
                    t_pt = P.op(
                        "scalar",
                        lambda e, par=par, par4=par4, dpar=dpar, kt=kt:
                        e.activation(
                            out=ptfin[:, par4, :], in_=Sp[par][:, :],
                            func=AF.Exp, bias=dvq[:, dpar, kt:kt + 1],
                            scale=1.0),
                        waits=[t_mm, t_dvq, pt_free[par4]], inc="sc")
                    S_free[par] = t_pt
                elif ms < 0:
                    t_ex = P.op(
                        "scalar",
                        lambda e, par=par: e.activation(
                            out=ptraw[:, par, :], in_=Sp[par][:, :],
                            func=AF.Exp),
                        waits=[t_mm, praw_free[par]], inc="sc")
                    S_free[par] = t_ex
                    t_pt = P.op(
                        "vector",
                        lambda e, par=par, par4=par4: e.tensor_tensor(
                            ptfin[:, par4, :], ptraw[:, par, :], hBt[:, :],
                            ALU.mult),
                        waits=[t_ex, t_hb, pt_free[par4]], inc="ve")
                    praw_free[par] = t_pt
                else:
                    t_ut = P.op(
                        "vector",
                        lambda e, par=par, dpar=dpar, kt=kt:
                        e.scalar_tensor_tensor(
                            ut[:, par, :], lJ[:, :], dvq[:, dpar, kt:kt + 1],
                            Sp[par][:, :], ALU.min, ALU.add),
                        waits=[t_mm, t_dvq, t_lj, ut_free[par]], inc="ve")
                    S_free[par] = t_ut
                    t_pt = P.op(
                        "scalar",
                        lambda e, par=par, par4=par4: e.activation(
                            out=ptfin[:, par4, :], in_=ut[:, par, :],
                            func=AF.Exp),
                        waits=[t_ut, pt_free[par4]], inc="sc")
                    ut_free[par] = t_pt
                av_pend.append((kt, par4, t_pt))
                if len(av_pend) > 2:
                    emit_av(*av_pend.pop(0))
            while av_pend:
                emit_av(*av_pend.pop(0))
            if sub == 1:
                scores_done[gpar] = P.tok("mm")
            t_av = P.tok("mm")
            t_tm = P.op("vector",
                        lambda e: e.tensor_copy(tm[0:65, :], pOT[0][:, :]),
                        waits=[t_av, tm_free], inc="ve")
            opar = (2 * g + sub) % 2
            t_mg = P.op("vector",
                        lambda e, opar=opar: e.scalar_tensor_tensor(
                            otm[0:65, opar, :], pOT[1][:, :], 1.0,
                            tm[0:65, :], ALU.bypass, ALU.add),
                        waits=[t_tm, otm_free[opar]], inc="ve")
            tm_free = t_mg
            pOT_free = [t_tm, t_mg]
            P.dma("dy", sc["ot"][64 * h:64 * (h + 1),
                                 1024 * qb4:1024 * (qb4 + 1)],
                  otm[0:64, opar, :], waits=[t_mg])
            P.dma("dy", sc["dd"][h * NQ4 + qb4:h * NQ4 + qb4 + 1, :],
                  otm[64:65, opar, :], waits=[t_mg])
            otm_free[opar] = P.mark("dy")
            dvq_free[dpar] = [P.tok("sc"), P.tok("ve")]
            if qb4 == NQ4 - 1:
                prep_free["bvD"] = t_dvq
                prep_free["lJ"] = [P.tok("ve"), t_hb]
                prep_free["hB"] = P.tok("ve")
    P.dma("dy", dall[:, :], sc["dd"][:, :], waits=[P.tok("dy")])
    t_da = P.mark("dy")
    t_l1 = P.op("scalar",
                lambda e: e.activation(out=lnd[:, :], in_=dall[:, :],
                                       func=AF.Ln),
                waits=[t_da], inc="sc")
    t_l2 = P.op("vector",
                lambda e: e.tensor_scalar(lnd[:, :], lnd[:, :], -1.0, None,
                                          ALU.mult),
                waits=[t_l1], inc="ve")
    t_l3 = P.op("scalar",
                lambda e: e.activation(out=rall[:, :], in_=lnd[:, :],
                                       func=AF.Exp),
                waits=[t_l2], inc="sc")
    P.dma("dy", sc["rr"][:, :], rall[:, :], waits=[t_l3])
    P.mark("dy")
    cnt = P.emit(nc, sems, "blkB")
    ge = dict(P.group_end)
    esB.close()

    P = Prog(cnt, ge)
    esB2 = ExitStack()
    cv = Carver(ktflat, 8 * max(T, 4096))
    otb = cv.take([128, 2, 1024], BF16)
    tst = cv.take([128, 1024], BF16)
    tsq = cv.take([128, 1024], F32)
    rbuf = cv.take([1, 2, 2, 1024], BF16)
    pRb = esB2.enter_context(nc.psum_tensor("pRb", [128, 1024], F32))
    pS1 = esB2.enter_context(nc.psum_tensor("pS1", [128, 16], F32))
    pS2 = esB2.enter_context(nc.psum_tensor("pS2", [1, 32], F32))

    rsc4 = sc["rr"].rearrange("(h q) d -> h q d", q=NQ4)

    def rb_build(Pp, pRbT, rbufs, waits):
        def rbfn(e, rbufs=rbufs):
            for nh in range(2):
                e.matmul(pRbT[0:64, 512 * nh:512 * (nh + 1)],
                         cb["cOnes"][0:1, 0:64],
                         rbufs[0:1, 0, 512 * nh:512 * (nh + 1)],
                         start=True, stop=True, tile_position=(0, 0))
            last = None
            for nh in range(2):
                last = e.matmul(pRbT[64:128, 512 * nh:512 * (nh + 1)],
                                cb["cOnes"][0:1, 0:64],
                                rbufs[0:1, 1, 512 * nh:512 * (nh + 1)],
                                start=True, stop=True,
                                tile_position=(0, 64))
            return last
        return Pp.op("tensor", rbfn, waits=waits, inc="mm")

    t_rall = P.tok("sc")
    rb_used = [Z, Z]
    otb_free = [Z, Z]
    ot_tok = {}
    P.dma("dy", otb[:, 0, :], sc["ot"][0:128, 0:1024], waits=[t_rall])
    ot_tok[0] = P.mark("dy")
    Rb_free = Z
    tst_free = Z
    tsq_free = Z
    t_acc = Z
    for g in range(8 * NQ4):
        c, qb4 = divmod(g, NQ4)
        par = g % 2
        if g + 1 < 8 * NQ4:
            nc_, nqb = divmod(g + 1, NQ4)
            P.dma("dy", otb[:, (g + 1) % 2, :],
                  sc["ot"][128 * nc_:128 * (nc_ + 1),
                           1024 * nqb:1024 * (nqb + 1)],
                  waits=[otb_free[(g + 1) % 2], ot_tok[g]])
            ot_tok[g + 1] = P.mark("dy")
        P.dma("dy", rbuf[0:1, g % 2, :, :], rsc4[2 * c:2 * c + 2, qb4, :],
              waits=[rb_used[g % 2]] if rb_used[g % 2][1] > 0 else [t_rall])
        t_rl = P.mark("dy")
        t_rb = rb_build(P, pRb, rbuf[0:1, g % 2, :, :],
                        [Rb_free, t_rl])
        t_t = P.op("vector",
                   lambda e, par=par: e.tensor_tensor(
                       tst[:, :], otb[:, par, :], pRb[:, :], ALU.mult),
                   waits=[t_rb, ot_tok[g], tst_free], inc="ve")
        Rb_free = t_t
        otb_free[par] = t_t
        rb_used[g % 2] = t_rb
        t_sq = P.op("scalar",
                    lambda e: e.activation(out=tsq[:, :], in_=tst[:, :],
                                           func=AF.Square),
                    waits=[t_t, tsq_free], inc="sc")
        t_r1 = P.op("vector",
                    lambda e: e.reduce_sum(smal[:, 30:31], tst[:, :],
                                           axis=AXX),
                    waits=[t_t], inc="ve")
        tst_free = t_r1
        t_r2 = P.op("vector",
                    lambda e: e.reduce_sum(smal[:, 31:32], tsq[:, :],
                                           axis=AXX),
                    waits=[t_sq], inc="ve")
        tsq_free = t_r2
        P.op("vector",
             lambda e, c=c: e.tensor_tensor(
                 stat[:, c:c + 1], stat[:, c:c + 1], smal[:, 30:31],
                 ALU.add),
             waits=[t_r1, t_acc], inc="ve")
        t_acc = P.op("vector",
                     lambda e, c=c: e.tensor_tensor(
                         stat[:, 8 + c:9 + c], stat[:, 8 + c:9 + c],
                         smal[:, 31:32], ALU.add),
                     waits=[t_r2, t_acc], inc="ve")
    t_st = P.op("vector", lambda e: e.tensor_copy(smal[:, 32:48], stat[:, :]),
                waits=[t_acc], inc="ve")

    def indmm(e):
        e.matmul(pS2[0:1, 0:16], cb["cInd0"][:, :], smal[:, 32:48],
                 start=True, stop=True)
        return e.matmul(pS2[0:1, 16:32], cb["cInd1"][:, :], smal[:, 32:48],
                        start=True, stop=True)
    t_ind = P.op("tensor", indmm, waits=[t_st, tok_consts], inc="mm")
    sclh = 1.0 / (64.0 * T)
    t_row = P.op("vector",
                 lambda e: e.tensor_scalar(srow[0:1, 0:32], pS2[0:1, 0:32],
                                           sclh, None, ALU.mult),
                 waits=[t_ind], inc="ve")
    t_s1 = P.op("vector",
                lambda e: e.tensor_tensor(srow[0:1, 32:40], srow[0:1, 0:8],
                                          srow[0:1, 0:8], ALU.mult),
                waits=[t_row], inc="ve")
    t_s2 = P.op("vector",
                lambda e: e.tensor_tensor(srow[0:1, 40:48], srow[0:1, 16:24],
                                          srow[0:1, 16:24], ALU.mult),
                waits=[t_row], inc="ve")
    t_s3 = P.op("vector",
                lambda e: e.scalar_tensor_tensor(
                    srow[0:1, 48:56], srow[0:1, 32:40], -1.0,
                    srow[0:1, 8:16], ALU.mult, ALU.add),
                waits=[t_s1], inc="ve")
    t_s4 = P.op("vector",
                lambda e: e.scalar_tensor_tensor(
                    srow[0:1, 56:64], srow[0:1, 40:48], -1.0,
                    srow[0:1, 24:32], ALU.mult, ALU.add),
                waits=[t_s2], inc="ve")
    t_s5 = P.op("scalar",
                lambda e: e.activation(out=srow[0:1, 32:48],
                                       in_=srow[0:1, 48:64], func=AF.Ln,
                                       bias=1e-5, scale=1.0),
                waits=[t_s3, t_s4], inc="sc")
    t_s6 = P.op("scalar",
                lambda e: e.activation(out=srow[0:1, 48:64],
                                       in_=srow[0:1, 32:48], func=AF.Exp,
                                       scale=-0.5),
                waits=[t_s5], inc="sc")
    t_s7 = P.op("vector",
                lambda e: e.scalar_tensor_tensor(
                    srow[0:1, 32:40], srow[0:1, 0:8], -1.0,
                    srow[0:1, 48:56], ALU.mult, ALU.mult),
                waits=[t_s6], inc="ve")
    t_s8 = P.op("vector",
                lambda e: e.scalar_tensor_tensor(
                    srow[0:1, 40:48], srow[0:1, 16:24], -1.0,
                    srow[0:1, 56:64], ALU.mult, ALU.mult),
                waits=[t_s6, t_s7], inc="ve")
    t_s9 = P.op("vector",
                lambda e: e.tensor_copy(srbf[0:1, 0:32], srow[0:1, 32:64]),
                waits=[t_s8], inc="ve")
    t_b1 = P.op("tensor",
                lambda e: e.matmul(pS1[:, :], cb["cOnes"][0:1, :],
                                   srbf[0:1, 16:32], start=True, stop=True),
                waits=[t_s9], inc="mm")
    t_b2_ = P.op("tensor",
                 lambda e: e.matmul(pRb[:, 0:16], cb["cOnes"][0:1, :],
                                    srbf[0:1, 0:16], start=True, stop=True),
                 waits=[t_s9, Rb_free], inc="mm")
    t_w1 = P.op("vector", lambda e: e.tensor_copy(smal[:, 0:16], pS1[:, :]),
                waits=[t_b1], inc="ve")
    t_w2 = P.op("vector",
                lambda e: e.tensor_copy(smal[:, 16:32], pRb[:, 0:16]),
                waits=[t_b2_], inc="ve")
    def gfn(e):
        for c in range(8):
            e.tensor_scalar(gnsc[0:64, c:c + 1], pp["gamma"][0:64, c:c + 1],
                            smal[0:64, c:c + 1], None, ALU.mult)
            e.tensor_scalar(gnsc[64:128, c:c + 1],
                            pp["gamma"][64:128, c:c + 1],
                            smal[64:128, 8 + c:9 + c], None, ALU.mult)
        return e.engine_nop()
    t_g1 = P.op("vector", gfn, waits=[t_w1, tok_consts], inc="ve")

    def gfn2(e):
        for c in range(8):
            e.scalar_tensor_tensor(gnbi[0:64, c:c + 1],
                                   pp["gamma"][0:64, c:c + 1],
                                   smal[0:64, 16 + c:17 + c],
                                   pp["beta"][0:64, c:c + 1], ALU.mult,
                                   ALU.add)
            e.scalar_tensor_tensor(gnbi[64:128, c:c + 1],
                                   pp["gamma"][64:128, c:c + 1],
                                   smal[64:128, 24 + c:25 + c],
                                   pp["beta"][64:128, c:c + 1], ALU.mult,
                                   ALU.add)
        return e.engine_nop()
    P.op("vector", gfn2, waits=[t_g1, t_w2], inc="ve")
    cnt = P.emit(nc, sems, "blkB2")
    ge = dict(P.group_end)
    esB2.close()

    P = Prog(cnt, ge)
    esB3 = ExitStack()
    cv = Carver(ktflat, 8 * max(T, 4096))
    wot = cv.take([128, 8, D], BF16)
    otb = cv.take([128, 2, 1024], BF16)
    gtb = cv.take([128, 2, 1024], BF16)
    tst = cv.take([128, 1024], BF16)
    pts = cv.take([128, 8, 1024], BF16)
    xr3 = cv.take([128, 2, 1024], BF16)
    sln = cv.take([128, 1024], F32)
    tsq = cv.take([128, 1024], F32)
    rbuf = cv.take([1, 2, 2, 1024], BF16)
    pRb3 = esB3.enter_context(nc.psum_tensor("pRb3", [128, 1024], F32))
    pOP = [esB3.enter_context(nc.psum_tensor(f"pOP{i}", [128, 1024], F32))
           for i in range(2)]

    tok_ffnB = ("dg", cnt["dg"])
    for dc in range(8):
        P.dma("dc", wot[:, dc, :],
              sc["woT"][128 * dc:128 * (dc + 1), :],
              waits=[tok_wT["wo"]])
    t_wot = P.mark("dc")
    t_gn = P.tok("ve")
    Rb_free = Z
    rb_used = [Z, Z]
    stream_free = [Z, Z]
    outproj_done = {}
    xr_free = [Z, Z]
    sln_free = Z
    tsq_free = Z
    x2_tok = {}
    pOP_free = [Z, Z]

    for qb4 in range(NQ4):
        for c in range(8):
            gi = qb4 * 8 + c
            par = gi % 2
            gate = [stream_free[par]] if stream_free[par][1] > 0 else \
                [t_gn]
            P.dma("dy", otb[:, par, :],
                  sc["ot"][128 * c:128 * (c + 1),
                           1024 * qb4:1024 * (qb4 + 1)],
                  waits=gate)
            P.dma("dy", gtb[:, par, :],
                  sc["gt"][128 * c:128 * (c + 1),
                           1024 * qb4:1024 * (qb4 + 1)])
            P.dma("dy", rbuf[0:1, gi % 2, :, :],
                  rsc4[2 * c:2 * c + 2, qb4, :],
                  waits=[rb_used[gi % 2]] if rb_used[gi % 2][1] > 0
                  else [t_gn])
            t_gt = P.mark("dy")
            t_ot = t_gt
            t_rb = rb_build(P, pRb3, rbuf[0:1, gi % 2, :, :],
                            [Rb_free, t_gt])
            t_t = P.op("vector",
                       lambda e, par=par: e.tensor_tensor(
                           tst[:, :], otb[:, par, :], pRb3[:, :], ALU.mult),
                       waits=[t_rb, t_ot, P.tok("ve")], inc="ve")
            Rb_free = t_t
            rb_used[gi % 2] = t_rb
            t_on = P.op("vector",
                        lambda e, c=c: e.tensor_scalar(
                            tst[:, :], tst[:, :], gnsc[:, c:c + 1],
                            gnbi[:, c:c + 1], ALU.mult, ALU.add),
                        waits=[t_t, t_gn], inc="ve")
            t_p = P.op("vector",
                       lambda e, c=c, par=par: e.tensor_tensor(
                           pts[:, c, :], tst[:, :], gtb[:, par, :],
                           ALU.mult),
                       waits=[t_on, t_gt, outproj_done.get(qb4 - 1, Z)],
                       inc="ve")
            stream_free[par] = t_p
        t_pts = P.tok("ve")
        for tcl in range(8):
            tc = 8 * qb4 + tcl
            xpar = tcl % 2
            xg = [xr_free[xpar]] if xr_free[xpar][1] > 0 else [tok_ffnB]
            P.dma("dg", xr3[:, xpar, :], ap["x"][128 * tc:128 * (tc + 1), :],
                  waits=xg)
            t_xr = P.mark("dg")

            opar2 = tcl % 2

            def opfn(e, tcl=tcl, opar2=opar2):
                for c in range(8):
                    for nh in range(2):
                        e.matmul(pOP[opar2][:, 512 * nh:512 * (nh + 1)],
                                 pts[:, c, 128 * tcl:128 * (tcl + 1)],
                                 wot[:, c, 512 * nh:512 * (nh + 1)],
                                 start=(c == 0), stop=False)
                last = None
                for nh in range(2):
                    last = e.matmul(pOP[opar2][:, 512 * nh:512 * (nh + 1)],
                                    cb["cOnes"][0:1, :],
                                    bo_row[0:1, 512 * nh:512 * (nh + 1)],
                                    start=False, stop=True)
                return last
            t_op = P.op("tensor", opfn, waits=[t_pts, t_wot,
                                               pOP_free[opar2],
                                               tok_S_dg], inc="mm")
            t_s = P.op("vector",
                       lambda e, xpar=xpar, opar2=opar2:
                       e.scalar_tensor_tensor(
                           sln[:, :], pOP[opar2][:, :], 1.0,
                           xr3[:, xpar, :], ALU.bypass, ALU.add),
                       waits=[t_op, t_xr, sln_free], inc="ve")
            pOP_free[opar2] = t_s
            xr_free[xpar] = t_s
            t_sq = P.op("scalar",
                        lambda e: e.activation(out=tsq[:, :], in_=sln[:, :],
                                               func=AF.Square),
                        waits=[t_s, tsq_free], inc="sc")
            t_r1 = P.op("vector",
                        lambda e: e.reduce_sum(smal[:, 40:41], sln[:, :],
                                               axis=AXX),
                        waits=[t_s], inc="ve")
            t_r2 = P.op("vector",
                        lambda e: e.reduce_sum(smal[:, 41:42], tsq[:, :],
                                               axis=AXX),
                        waits=[t_sq], inc="ve")
            tsq_free = t_r2
            t_m = P.op("vector",
                       lambda e: e.tensor_scalar(smal[:, 42:44],
                                                 smal[:, 40:42], 1.0 / D,
                                                 None, ALU.mult),
                       waits=[t_r1, t_r2], inc="ve")
            t_m2 = P.op("vector",
                        lambda e: e.scalar_tensor_tensor(
                            smal[:, 44:45], smal[:, 42:43], smal[:, 42:43],
                            smal[:, 43:44], ALU.mult, ALU.subtract),
                        waits=[t_m], inc="ve")
            t_ln = P.op("scalar",
                        lambda e: e.activation(out=smal[:, 45:46],
                                               in_=smal[:, 44:45],
                                               func=AF.Ln, bias=1e-5,
                                               scale=-1.0),
                        waits=[t_m2], inc="sc")
            t_rs = P.op("scalar",
                        lambda e: e.activation(out=smal[:, 46:47],
                                               in_=smal[:, 45:46],
                                               func=AF.Exp, scale=-0.5),
                        waits=[t_ln], inc="sc")
            t_nm = P.op("vector",
                        lambda e: e.tensor_scalar(
                            smal[:, 47:48], smal[:, 42:43], smal[:, 46:47],
                            -1.0, ALU.mult, ALU.mult),
                        waits=[t_rs], inc="ve")
            x2_tok[tc] = P.op("vector",
                              lambda e, tc=tc: e.tensor_scalar(
                                  x2v[:, tc, :], sln[:, :], smal[:, 46:47],
                                  smal[:, 47:48], ALU.mult, ALU.add),
                              waits=[t_nm], inc="ve")
            sln_free = x2_tok[tc]
        outproj_done[qb4] = P.tok("mm")
    cnt = P.emit(nc, sems, "blkB3")
    ge = dict(P.group_end)
    esB3.close()

    NTC_ = T // 128
    P = Prog(cnt, ge)
    esC = ExitStack()
    cv = Carver(ktflat, 8 * max(T, 4096))
    xt2 = cv.take([128, 8, 256], BF16)
    rwt = cv.take([128, 8, NE], BF16)
    lg = cv.take([128, NE], F32)
    mkb = cv.take([128, NE], BF16)
    mtb = cv.take([32, 128], F32)
    csb = cv.take([32, 128], F32)
    msl = cv.take([32, 128], F32)
    base = cv.take([32, 2], F32)
    drow = cv.take([1, 128], F32)
    idxall = cv.take([128, 32], mybir.dt.uint32)
    xst = cv.take([128, 8, 256], BF16)
    hts = cv.take([128, 2, 512], BF16)
    ysb = cv.take([128, 2, 1024], BF16)
    b2r = cv.take([1, 2, 1024], BF16)
    ygb = cv.take([128, 2, 1024], BF16)
    sln = cv.take([128, 1024], F32)
    tsq = cv.take([128, 1024], F32)
    oout = cv.take([128, 2, 1024], F32)
    xsr = cv.take([128, 4, 1024], BF16)
    cvv = Carver(vp[:, :, :, :].rearrange("p a b c -> p (a b c)"),
                 (4096 // 128) * H * 65)
    w1e = cvv.take([128, 4, 8, HF], BF16)
    w2e = cvv.take([128, 4, 2, D], BF16)

    pz = [esC.enter_context(nc.psum_tensor(f"pz{i}", [128, 1024], F32))
          for i in range(2)]
    ph = [esC.enter_context(nc.psum_tensor(f"ph{i}", [128, 512], F32))
          for i in range(2)]
    pmb = esC.enter_context(nc.psum_tensor("pmb", [128, 256], BF16))
    pmf = esC.enter_context(nc.psum_tensor("pmf", [128, 256], F32))

    for dc in range(8):
        P.dma("dc", rwt[:, dc, :], sc["rw"][:, 128 * dc:128 * (dc + 1)],
              transpose=True, waits=[tok_wstage])
    t_rwt = P.mark("dc")
    tok_ffnC = ("dg", cnt["dg"])
    t_bz = P.op("vector", lambda e: e.memset(base[:, :], 0.0), inc="ve")

    pmb_free = Z
    pmf_free = Z
    ve_last = Z
    sc_tok = {}
    for tc in range(NTC_):
        ev_prev = pmb_free
        for dc in range(8):
            t_tr = P.op("tensor",
                        lambda e, tc=tc, dc=dc: e.transpose(
                            pmb[:, 0:128],
                            x2v[:, tc, 128 * dc:128 * (dc + 1)],
                            cb["cIdent"][:, :]),
                        waits=[x2_tok.get(tc, Z), ev_prev], inc="mm")
            ev_prev = P.op("vector",
                           lambda e, dc=dc: e.tensor_copy(
                               xt2[:, dc, 0:128], pmb[:, 0:128]),
                           waits=[t_tr], inc="ve")
        t_xt2 = P.tok("ve")
        pmb_free = ev_prev

        def rmm(e):
            for dc in range(8):
                e.matmul(pmf[:, 0:NE], xt2[:, dc, 0:128],
                         rwt[:, dc, :], start=(dc == 0), stop=False)
            return e.matmul(pmf[:, 0:NE], cb["cOnes"][0:1, :],
                            rb_row[0:1, :], start=False, stop=True)
        t_rmm = P.op("tensor", rmm, waits=[t_xt2, t_rwt, pmf_free,
                                           tok_S_dg], inc="mm")
        t_lg = P.op("vector",
                    lambda e: e.tensor_copy(lg[:, :], pmf[:, 0:NE]),
                    waits=[t_rmm], inc="ve")
        t_mx = P.op("vector",
                    lambda e: e.reduce_max(smal[:, 50:51], lg[:, :],
                                           axis=AXX),
                    waits=[t_lg], inc="ve")
        t_mk = P.op("vector",
                    lambda e: e.tensor_scalar(mkb[:, :], lg[:, :],
                                              smal[:, 50:51], None,
                                              ALU.is_equal),
                    waits=[t_mx], inc="ve")
        t_mt = P.op("tensor",
                    lambda e: e.transpose(pmb[0:32, 0:128], mkb[:, :],
                                          cb["cIdent"][:, :]),
                    waits=[t_mk, pmb_free], inc="mm")
        t_mte = P.op("vector",
                     lambda e: e.tensor_copy(mtb[:, :], pmb[0:32, 0:128]),
                     waits=[t_mt], inc="ve")
        pmb_free = t_mte
        t_tri = P.op("tensor",
                     lambda e: e.matmul(pmf[0:32, 128:256], mkb[:, :],
                                        cb["cLT"][:, :], start=True,
                                        stop=True),
                     waits=[t_mk, t_lg], inc="mm")
        t_cs = P.op("vector",
                    lambda e: e.tensor_scalar(csb[:, :], pmf[0:32, 128:256],
                                              base[:, 0:1], -1.0, ALU.add,
                                              ALU.add),
                    waits=[t_tri, t_bz, ve_last], inc="ve")
        t_cl = P.op("vector",
                    lambda e: e.tensor_scalar(csb[:, :], csb[:, :],
                                              float(CAP - 1), None, ALU.min),
                    waits=[t_cs], inc="ve")
        t_bu = P.op("vector",
                    lambda e: e.tensor_tensor(base[:, 0:1], base[:, 0:1],
                                              pmf[0:32, 255:256], ALU.add),
                    waits=[t_cs], inc="ve")
        t_bz = t_bu
        t_ad = P.op("vector",
                    lambda e: e.tensor_scalar(csb[:, :], csb[:, :],
                                              cb["cIotaC"][:, :], None,
                                              ALU.add),
                    waits=[t_cl, tok_consts], inc="ve")
        t_ms = P.op("vector",
                    lambda e: e.tensor_tensor(msl[:, :], csb[:, :],
                                              mtb[:, :], ALU.mult),
                    waits=[t_ad, t_mte], inc="ve")
        t_rr = P.op("tensor",
                    lambda e: e.matmul(pmf[0:1, 128:256],
                                       cb["cInd0"][0:32, :], msl[:, :],
                                       start=True, stop=True),
                    waits=[t_ms, t_bu, tok_consts], inc="mm")
        t_dc2 = P.op("vector",
                     lambda e: e.tensor_scalar(drow[0:1, :],
                                               pmf[0:1, 128:256],
                                               float(NE * CAP - 1), None,
                                               ALU.min),
                     waits=[t_rr], inc="ve")
        t_dt = P.op("tensor",
                    lambda e: e.transpose(pmf[:, 250:251], drow[0:1, :],
                                          cb["cIdF"][0:1, 0:1]),
                    waits=[t_dc2], inc="mm")
        t_ix = P.op("vector",
                    lambda e, tc=tc: e.tensor_copy(idxall[:, tc:tc + 1],
                                                   pmf[:, 250:251]),
                    waits=[t_dt], inc="ve")
        pmf_free = t_ix
        ve_last = t_ix
        P.op("gpsimd",
             lambda e, tc=tc: e.indirect_dma_start(
                 sc["xs"][:, :],
                 IndirectOffsetOnAxis(idxall[:, tc:tc + 1], 0),
                 x2v[:, tc, :], None),
             waits=[t_ix, tok_ffnC, ("dg", P.group_end.get("dg", 0))],
             inc="dg", incby=16)
        sc_tok[tc] = P.mark("dg")
    tok_scatter = P.tok("dg")

    w1_used = [Z, Z, Z, Z]
    w2_used = [Z, Z, Z, Z]
    we_tok = {}

    def load_expert(P, eL):
        slot = eL % 4
        ne = eL % NE
        s1 = "wa" if slot % 2 == 0 else "wb"
        s2 = "wc" if slot % 2 == 0 else "wd2"
        wg_ = [w1_used[slot], w2_used[slot]]
        if all(t[1] == 0 for t in wg_):
            wg_ = [tok_ffnC]
        P.dma(s1, w1e[:, slot, :, :],
              sc["w1"][ne, :, :].rearrange("(p c) f -> p c f", c=8),
              waits=wg_)
        ta_ = P.mark(s1)
        P.dma(s2, w2e[:, slot, :, :],
              sc["w2"][ne, :, :].rearrange("(p c) f -> p c f", c=2),
              waits=wg_)
        we_tok[eL] = [ta_, P.mark(s2)]

    load_expert(P, 0)
    load_expert(P, 1)
    b2_tok = {}
    b2_rd = {}
    P.dma("dy", b2r[0:1, 0, :], sc["b2"][0:1, :], waits=[tok_wstage])
    b2_tok[0] = P.mark("dy")
    xs_free = [Z, Z, Z, Z]
    xst_free = Z
    ph_free = [Z, Z]
    hts_free = [Z, Z]
    pz_free = [Z, Z]
    ysb_free = [Z, Z]
    for e_ in range(NE):
        slot = e_ % 4
        if e_ + 2 < NE:
            load_expert(P, e_ + 2)
        if e_ + 1 < NE:
            bw = [b2_rd[e_ - 1]] if e_ - 1 in b2_rd else []
            P.dma("dy", b2r[0:1, (e_ + 1) % 2, :],
                  sc["b2"][e_ + 1:e_ + 2, :], waits=bw)
            b2_tok[e_ + 1] = P.mark("dy")
        if True:
            xp = 2 * (e_ % 2)
            for sb_ in range(2):
                r0 = e_ * CAP + 128 * sb_
                P.dma("dy", xsr[:, xp + sb_, :], sc["xs"][r0:r0 + 128, :],
                      waits=[tok_scatter, xs_free[xp + sb_]])
            t_xs = P.mark("dy")
            ev_prev = pmb_free
            for i in range(16):
                sb_, dc = i // 8, i % 8
                t_tr = P.op("tensor",
                            lambda e, sb_=xp + sb_, dc=dc: e.transpose(
                                pmb[:, 0:128],
                                xsr[:, sb_, 128 * dc:128 * (dc + 1)],
                                cb["cIdent"][:, :]),
                            waits=[t_xs, xst_free, ev_prev], inc="mm")
                ev_prev = P.op("vector",
                               lambda e, sb_=sb_, dc=dc: e.tensor_copy(
                                   xst[:, dc, 128 * sb_:128 * (sb_ + 1)],
                                   pmb[:, 0:128]),
                               waits=[t_tr], inc="ve")
            t_xst = P.tok("ve")
            pmb_free = ev_prev
            xs_free[xp] = t_xst
            xs_free[xp + 1] = t_xst
            ppar = e_ % 2

            def w1mm(e, slot=slot, ppar=ppar):
                last = None
                for hh in range(2):
                    for dc in range(8):
                        last = e.matmul(
                            ph[ppar][:, 256 * hh:256 * (hh + 1)],
                            w1e[:, slot, dc, 128 * hh:128 * (hh + 1)],
                            xst[:, dc, :],
                            start=(dc == 0), stop=(dc == 7))
                return last
            t_w1 = P.op("tensor", w1mm,
                        waits=we_tok[e_] + [t_xst, ph_free[ppar]], inc="mm")
            w1_used[slot] = t_w1
            xst_free = t_w1

            def ev1(e, ppar=ppar, ee=e_):
                e.activation(out=hts[:, ppar, 0:256],
                             in_=ph[ppar][:, 0:256], func=AF.Relu,
                             bias=b1_pp[:, ee, 0:1], scale=1.0)
                return e.activation(out=hts[:, ppar, 256:512],
                                    in_=ph[ppar][:, 256:512], func=AF.Relu,
                                    bias=b1_pp[:, ee, 1:2], scale=1.0)
            t_h = P.op("scalar", ev1,
                       waits=[t_w1, hts_free[ppar], tok_consts], inc="sc")
            ph_free[ppar] = t_h

            def w2mm(e, slot=slot, ppar=ppar, e_=e_):
                last = None
                for tsub in range(2):
                    for kc in range(2):
                        for nh in range(2):
                            last = e.matmul(
                                pz[tsub][:, 512 * nh:512 * (nh + 1)],
                                hts[:, ppar,
                                    256 * kc + 128 * tsub:
                                    256 * kc + 128 * (tsub + 1)],
                                w2e[:, slot, kc, 512 * nh:512 * (nh + 1)],
                                start=(kc == 0), stop=False)
                    for nh in range(2):
                        last = e.matmul(
                            pz[tsub][:, 512 * nh:512 * (nh + 1)],
                            cb["cOnes"][0:1, :],
                            b2r[0:1, e_ % 2, 512 * nh:512 * (nh + 1)],
                            start=False, stop=True)
                return last
            t_z = P.op("tensor", w2mm,
                       waits=[t_h, b2_tok[e_], pz_free[0], pz_free[1]],
                       inc="mm")
            w2_used[slot] = t_z
            hts_free[ppar] = t_z
            b2_rd[e_] = t_z
            for tsub in range(2):
                t_ye = P.op("scalar",
                            lambda e, tsub=tsub: e.activation(
                                out=ysb[:, tsub, :], in_=pz[tsub][:, :],
                                func=AF.Relu),
                            waits=[t_z, ysb_free[tsub]], inc="sc")
                pz_free[tsub] = t_ye
                r0 = e_ * CAP + 128 * tsub
                P.dma("dy", sc["ys"][r0:r0 + 128, :], ysb[:, tsub, :],
                      waits=[t_ye])
                ysb_free[tsub] = P.mark("dy")
    tok_ys = P.tok("dy")

    ygb_free = [Z, Z]
    sln_free = Z
    tsq_free = Z
    oout_free = [Z, Z]
    st_count = 0
    for tc in range(NTC_):
        par = tc % 2
        P.op("gpsimd",
             lambda e, tc=tc, par=par: e.indirect_dma_start(
                 ygb[:, par, :], None,
                 sc["ys"][:, :],
                 IndirectOffsetOnAxis(idxall[:, tc:tc + 1], 0)),
             waits=[tok_ys, ygb_free[par],
                    ("dg", P.group_end.get("dg", 0))],
             inc="dg", incby=16)
        t_yg = P.mark("dg")
        t_s = P.op("vector",
                   lambda e, tc=tc, par=par: e.tensor_tensor(
                       sln[:, :], ygb[:, par, :], x2v[:, tc, :], ALU.add),
                   waits=[t_yg, sln_free], inc="ve")
        ygb_free[par] = t_s
        t_sq = P.op("scalar",
                    lambda e: e.activation(out=tsq[:, :], in_=sln[:, :],
                                           func=AF.Square),
                    waits=[t_s, tsq_free], inc="sc")
        t_r1 = P.op("vector",
                    lambda e: e.reduce_sum(smal[:, 40:41], sln[:, :],
                                           axis=AXX),
                    waits=[t_s], inc="ve")
        t_r2 = P.op("vector",
                    lambda e: e.reduce_sum(smal[:, 41:42], tsq[:, :],
                                           axis=AXX),
                    waits=[t_sq], inc="ve")
        tsq_free = t_r2
        t_m = P.op("vector",
                   lambda e: e.tensor_scalar(smal[:, 42:44],
                                             smal[:, 40:42], 1.0 / D,
                                             None, ALU.mult),
                   waits=[t_r1, t_r2], inc="ve")
        t_m2 = P.op("vector",
                    lambda e: e.scalar_tensor_tensor(
                        smal[:, 44:45], smal[:, 42:43], smal[:, 42:43],
                        smal[:, 43:44], ALU.mult, ALU.subtract),
                    waits=[t_m], inc="ve")
        t_ln = P.op("scalar",
                    lambda e: e.activation(out=smal[:, 45:46],
                                           in_=smal[:, 44:45],
                                           func=AF.Ln, bias=1e-5,
                                           scale=-1.0),
                    waits=[t_m2], inc="sc")
        t_rs = P.op("scalar",
                    lambda e: e.activation(out=smal[:, 46:47],
                                           in_=smal[:, 45:46],
                                           func=AF.Exp, scale=-0.5),
                    waits=[t_ln], inc="sc")
        t_nm = P.op("vector",
                    lambda e: e.tensor_scalar(
                        smal[:, 47:48], smal[:, 42:43], smal[:, 46:47],
                        -1.0, ALU.mult, ALU.mult),
                    waits=[t_rs], inc="ve")
        t_o = P.op("vector",
                   lambda e, par=par: e.tensor_scalar(
                       oout[:, par, :], sln[:, :], smal[:, 46:47],
                       smal[:, 47:48], ALU.mult, ALU.add),
                   waits=[t_nm, oout_free[par]], inc="ve")
        sln_free = t_o
        P.dma("st", out[128 * tc:128 * (tc + 1), :], oout[:, par, :],
              waits=[t_o])
        st_count += 16
        oout_free[par] = P.mark("st")
    P.op("sync", lambda e: e.nop(), waits=[("st", st_count)])
    cnt = P.emit(nc, sems, "blkC")
    esC.close()
    es.close()
    return nc


_NC_CACHE = {}


def _get_program(T):
    if T not in _NC_CACHE:
        _NC_CACHE[T] = build_program(T)
    return _NC_CACHE[T]


def make_in_maps(inputs, T=4096):
    import ml_dtypes
    x = np.ascontiguousarray(np.asarray(inputs["L0_attn_x"],
                                        dtype=np.float32))
    consts = host_constants(T)
    base = {
        "wq": inputs["L0_attn_wq"], "bq": inputs["L0_attn_bq"],
        "wk": inputs["L0_attn_wk"], "bk": inputs["L0_attn_bk"],
        "wv": inputs["L0_attn_wv"], "bv": inputs["L0_attn_bv"],
        "wg": inputs["L0_attn_wg"], "bg": inputs["L0_attn_bg"],
        "wo": inputs["L0_attn_wo"], "bo": inputs["L0_attn_bo"],
        "wd": inputs["L0_attn_wd"], "bd": inputs["L0_attn_bd"],
        "gamma": inputs["L0_attn_gamma"], "beta": inputs["L0_attn_beta"],
        "rw": inputs["L0_ffn_router_w"], "rb": inputs["L0_ffn_router_b"],
        "w1": inputs["L0_ffn_weights1"], "b1": inputs["L0_ffn_bias1"],
        "w2": inputs["L0_ffn_weights2"], "b2": inputs["L0_ffn_bias2"],
    }
    base = {k: np.ascontiguousarray(np.asarray(v, dtype=np.float32))
            for k, v in base.items()}
    for k, v in consts.items():
        dt = CONST_SPECS[k][1]
        base[k] = (np.ascontiguousarray(v.astype(ml_dtypes.bfloat16))
                   if dt == BF16 else np.ascontiguousarray(
                       v.astype(np.float32)))
    ncore = x.shape[0]
    return [dict(base, x=np.ascontiguousarray(x[i])) for i in range(ncore)]


def kernel(**inputs):
    T = 4096
    nc = _get_program(T)
    in_maps = make_in_maps(inputs, T)
    res = run_bass_kernel_spmd(nc, in_maps, core_ids=list(range(NCORES)))
    return np.stack([np.asarray(r["out"], dtype=np.float32)
                     for r in res.results], axis=0)
